# version 10
# baseline (speedup 1.0000x reference)
import math
from contextlib import ExitStack

import numpy as np
import ml_dtypes
import concourse.bass as bass
import concourse.mybir as mybir
from concourse.bass_utils import run_bass_kernel_spmd

F32 = mybir.dt.float32
BF16 = mybir.dt.bfloat16
AF = mybir.ActivationFunctionType
ALU = mybir.AluOpType
AX = mybir.AxisListType.X

T = 2048
HB = 1024
NL = 4
NE = 4
DM = 1024
NH = 16
HS = 64
DR = 32
EPS = 1e-5
ISQ = 1.0 / math.sqrt(HS + DR)

bfnp = ml_dtypes.bfloat16
ENGS = ("tensor", "vector", "scalar", "gpsimd", "sync")
SEMS = ("PE", "V", "S", "P", "GEL", "EVC", "W1C", "W2C") + tuple(
    f"DW{j}" for j in range(8)) + tuple(f"DS{j}" for j in range(8))


class Sched:
    def __init__(self, nc):
        self.nc = nc
        self.thunks = {e: [] for e in ENGS}
        self.sems = {}
        self.counts = {}
        self.last_wait = {}
        self.ms = {}
        self.dma_idx = {"DW": 0, "DS": 0}
        self.last_inc = {}
        for sn in SEMS:
            self.sem(sn)

    def sem(self, name):
        if name not in self.sems:
            self.sems[name] = self.nc.semaphore(name).__enter__()
            self.counts[name] = 0
        return self.sems[name]

    def emit(self, eng, fn, inc=None, inc_n=1, tag="", inc2=None):
        if eng in ("vector", "scalar") and eng in self.last_inc:
            psem, pv = self.last_inc[eng]
            self.wait(eng, psem, pv, tag=f"chain_{tag}")
        if inc is None:
            self.thunks[eng].append({"fn": fn, "waits": [], "incs": [], "tag": tag})
            return None
        sem = self.sem(inc)
        self.counts[inc] += inc_n
        incs = [(inc, inc_n)]
        if inc2 is not None:
            sem2 = self.sem(inc2[0])
            self.counts[inc2[0]] += inc2[1]
            incs.append((inc2[0], inc2[1]))
            self.thunks[eng].append({
                "fn": (lambda e, fn=fn, sem=sem, n=inc_n, s2=sem2, n2=inc2[1]:
                       fn(e).then_inc(sem, n).then_inc(s2, n2)),
                "waits": [], "incs": incs, "tag": tag})
        else:
            self.thunks[eng].append({
                "fn": (lambda e, fn=fn, sem=sem, n=inc_n: fn(e).then_inc(sem, n)),
                "waits": [], "incs": incs, "tag": tag})
        if eng in ("vector", "scalar", "gpsimd"):
            self.last_inc[eng] = (inc, self.counts[inc])
        return self.counts[inc]

    def wait(self, eng, name, val, tag=""):
        if val is None or val <= 0:
            return
        key = (eng, name)
        if val <= self.last_wait.get(key, 0):
            return
        self.last_wait[key] = val
        sem = self.sem(name)
        self.thunks[eng].append({
            "fn": (lambda e, sem=sem, val=val: e.wait_ge(sem, val)),
            "waits": [(name, val)], "incs": [], "tag": tag})

    def mark(self, name, semname):
        self.ms[name] = (semname, self.counts[semname])

    def wait_tok(self, eng, tok, tag=""):
        if tok is None:
            return
        self.wait(eng, tok[0], tok[1], tag=tag)

    def wait_ms(self, eng, name, tag=""):
        semname, v = self.ms[name]
        self.wait(eng, semname, v, tag=tag or name)

    def fence(self, name):
        for sn in SEMS:
            self.ms[f"{name}:{sn}"] = (sn, self.counts[sn])

    def wait_fence(self, eng, name, tag=""):
        for sn in SEMS:
            self.wait_ms(eng, f"{name}:{sn}", tag=tag or name)

    def check(self):
        ptr = {e: 0 for e in ENGS}
        vals = {k: 0 for k in self.counts}
        progress = True
        while progress:
            progress = False
            for e in ENGS:
                while ptr[e] < len(self.thunks[e]):
                    th = self.thunks[e][ptr[e]]
                    if all(vals.get(sn, 0) >= v for sn, v in th["waits"]):
                        for sn, n in th["incs"]:
                            vals[sn] = vals.get(sn, 0) + n
                        ptr[e] += 1
                        progress = True
                    else:
                        break
        stuck = {e: ptr[e] for e in ENGS if ptr[e] < len(self.thunks[e])}
        if stuck:
            lines = []
            for e, p in stuck.items():
                th = self.thunks[e][p]
                lines.append(f"  {e}[{p}/{len(self.thunks[e])}] tag={th['tag']} waits={th['waits']}")
            raise RuntimeError("DEADLOCK in schedule:\n" + "\n".join(lines))

    def dispatch(self):
        with self.nc.Block() as block:
            @block.tensor
            def _(eng):
                for th in self.thunks["tensor"]:
                    th["fn"](eng)

            @block.vector
            def _(eng):
                for th in self.thunks["vector"]:
                    th["fn"](eng)

            @block.scalar
            def _(eng):
                for th in self.thunks["scalar"]:
                    th["fn"](eng)

            @block.gpsimd
            def _(eng):
                for th in self.thunks["gpsimd"]:
                    th["fn"](eng)

            @block.sync
            def _(eng):
                for th in self.thunks["sync"]:
                    th["fn"](eng)


def tile_ffn_weights(w1, w2):
    L, E = w1.shape[0], w1.shape[1]
    out = np.empty((L, E, 32, 128, 2048), np.float32)
    t1 = w1.reshape(L, E, 8, 128, 16, 256).transpose(0, 1, 4, 3, 2, 5)
    out[:, :, 0:16] = t1.reshape(L, E, 16, 128, 2048)
    t2 = w2.reshape(L, E, 2, 16, 128, 8, 128).transpose(0, 1, 5, 2, 4, 3, 6)
    out[:, :, 16:32] = t2.reshape(L, E, 16, 128, 2048)
    return np.ascontiguousarray(out)


def build_consts():
    c = {}
    c["id_f32"] = np.eye(128, dtype=np.float32)
    c["id_bf16"] = np.eye(128, dtype=bfnp)
    tri = np.where(np.arange(128)[:, None] >= np.arange(128)[None, :], 0.0, -1e30)
    c["tri"] = tri.astype(bfnp)
    ones = np.zeros((65, 128), dtype=bfnp)
    ones[0, :] = 1.0
    ones[32, :] = 1.0
    ones[64, :] = 1.0
    c["onesb"] = ones
    sc = np.zeros((128, 2), dtype=np.float32)
    sc[:, 0] = -1.0 / 1024.0
    sc[:, 1] = 1.0 / 1024.0
    c["statlhs"] = sc.astype(bfnp)
    oh = np.zeros((4, 4 * 128), dtype=bfnp)
    for e in range(4):
        oh[e, e * 128:(e + 1) * 128] = 1.0
    c["oh4"] = oh
    t = np.arange(T, dtype=np.float32)
    inv = (1.0 / 10000.0 ** (np.arange(0, DR, 2, dtype=np.float32) / DR)).astype(np.float32)
    freqs = t[:, None] * inv[None, :]
    emb = np.concatenate([freqs, freqs], -1).astype(np.float32)
    scale = (np.cos(emb) + np.sin(emb)).astype(np.float32)
    qrope = np.empty((96, T), np.float32)
    for p in range(96):
        qrope[p] = scale[:, p % 32]
    c["qrope"] = qrope.astype(bfnp)
    return c


def build_program(depth=NL, do_attn=True, do_ffn2=True, stop=None, dbg=False, dbg_call=0, chain_trunc=None):
    nc = bass.Bass("TRN2", target_bir_lowering=False)
    consts = build_consts()

    x_in = nc.dram_tensor("x_in", [T, DM], F32, kind="ExternalInput").ap()
    wt = nc.dram_tensor("wt", [NL, NE, 32, 128, 2048], F32, kind="ExternalInput").ap()
    gcol_d = nc.dram_tensor("gcol", [128, 128], F32, kind="ExternalInput").ap()
    bcol_d = nc.dram_tensor("bcol", [128, 128], F32, kind="ExternalInput").ap()
    b1c_d = nc.dram_tensor("b1c", [128, 512], F32, kind="ExternalInput").ap()
    b2c_d = nc.dram_tensor("b2c", [128, 128], F32, kind="ExternalInput").ap()
    rwc_d = nc.dram_tensor("rwc", [128, 128], F32, kind="ExternalInput").ap()
    rbc_d = nc.dram_tensor("rbc", [4, 4], F32, kind="ExternalInput").ap()
    wdq_d = nc.dram_tensor("wdq", [32, 128, 1024], F32, kind="ExternalInput").ap()
    wuq_d = nc.dram_tensor("wuq", [32, 128, 1024], F32, kind="ExternalInput").ap()
    wqr_d = nc.dram_tensor("wqr", [24, 128, 768], F32, kind="ExternalInput").ap()
    wdkv_d = nc.dram_tensor("wdkv", [2, 128, 1024], F32, kind="ExternalInput").ap()
    wuk_d = nc.dram_tensor("wuk", [128, 2048], F32, kind="ExternalInput").ap()
    wuv_d = nc.dram_tensor("wuv", [128, 2048], F32, kind="ExternalInput").ap()
    wkr_d = nc.dram_tensor("wkr", [128, 256], F32, kind="ExternalInput").ap()
    wo_d = nc.dram_tensor("wo", [8, 128, 1024], F32, kind="ExternalInput").ap()
    y_out = nc.dram_tensor("y_out", [T, DM], F32, kind="ExternalOutput").ap()
    if dbg:
        dbg_comb = nc.dram_tensor("dbg_comb", [4, T], F32, kind="ExternalOutput").ap()
        dbg_zhat = nc.dram_tensor("dbg_zhat", [128, 8 * HB], F32, kind="ExternalOutput").ap()
        dbg_xe = nc.dram_tensor("dbg_xe", [128, 8 * HB], F32, kind="ExternalOutput").ap()
        dbg_z = nc.dram_tensor("dbg_z", [128, 8 * T], F32, kind="ExternalOutput").ap()
    xspill = nc.dram_tensor("xspill", [128, 8 * T], F32, kind="Internal").ap()
    wt1_bf = nc.dram_tensor("wt1_bf", [NL, NE, 16, 128, 2048], BF16, kind="Internal").ap()
    wt2_bf = nc.dram_tensor("wt2_bf", [NL, NE, 16, 128, 2048], BF16, kind="Internal").ap()
    awq_bf = {}
    for nm, shp in [("wdq", [32, 128, 1024]), ("wuq", [32, 128, 1024]),
                    ("wqr", [24, 128, 768]), ("wdkv", [2, 128, 1024]),
                    ("wuk", [128, 2048]), ("wuv", [128, 2048]),
                    ("wkr", [128, 256]), ("wo", [8, 128, 1024])]:
        awq_bf[nm] = nc.dram_tensor(nm + "_bf", shp, BF16, kind="Internal").ap()

    cst = {k: nc.inline_tensor(v, f"c_{k}") for k, v in consts.items()}
    fconst_np = np.array([[EPS, -0.5, ISQ, 0.0]], np.float32)
    fconst_d = nc.inline_tensor(fconst_np, "c_fconst")

    with ExitStack() as top:
        Et = top.enter_context
        stream = Et(nc.sbuf_tensor("stream", [128, 8 * T], F32))
        rows = Et(nc.sbuf_tensor("rows", [66, T], BF16))
        id_f = Et(nc.sbuf_tensor("id_f", [128, 128], F32))
        id_b = Et(nc.sbuf_tensor("id_b", [128, 128], BF16))
        tri = Et(nc.sbuf_tensor("tri", [128, 128], BF16))
        onesb = Et(nc.sbuf_tensor("onesb", [65, 128], BF16))
        statlhs = Et(nc.sbuf_tensor("statlhs", [128, 2], BF16))
        oh4 = Et(nc.sbuf_tensor("oh4", [4, 512], BF16))
        smalls = Et(nc.sbuf_tensor("smalls", [128, 16], F32))
        fconst = Et(nc.sbuf_tensor("fconst", [1, 4], F32))
        nc.const_aps.register(F32, EPS, fconst[0:1, 0:1])
        nc.const_aps.register(F32, -0.5, fconst[0:1, 1:2])
        nc.const_aps.register(F32, ISQ, fconst[0:1, 2:3])

        s = Sched(nc)

        def dma(eng, out_ap, in_ap, group, tag=""):
            idx = s.dma_idx[group]
            s.dma_idx[group] = idx + 1
            lane = f"{group}{idx % 8}"
            s.wait(eng, lane, 16 * (idx // 8), tag=f"{tag}_pred")
            cnt = s.emit(eng, lambda e, o=out_ap, i=in_ap: e.dma_start(o, i),
                         inc=lane, inc_n=16, tag=tag)
            return (lane, cnt)

        const_toks = [
            dma("sync", id_f[:], cst["id_f32"].ap(), "DS", "c"),
            dma("sync", id_b[:], cst["id_bf16"].ap(), "DS", "c"),
            dma("sync", tri[:], cst["tri"].ap(), "DS", "c"),
            dma("sync", onesb[:], cst["onesb"].ap(), "DS", "c"),
            dma("sync", statlhs[:], cst["statlhs"].ap(), "DS", "c"),
            dma("sync", oh4[:], cst["oh4"].ap(), "DS", "c"),
            dma("sync", fconst[:], fconst_d.ap(), "DS", "c"),
        ]
        for tk in const_toks:
            s.wait_tok("tensor", tk, "consts")
            s.wait_tok("vector", tk, "consts")
            s.wait_tok("scalar", tk, "consts")

        with ExitStack() as sc0:
            xin = sc0.enter_context(nc.sbuf_tensor("xin", [128, 2 * DM], F32))
            psT0 = sc0.enter_context(nc.psum_tensor("psT0", [128, 1024], F32))
            vhist = []
            pehist = []
            for tt in range(16):
                sl = tt % 2
                if tt >= 2:
                    s.wait("sync", "PE", pehist[tt - 2], f"ix{tt}")
                dtok = dma("sync", xin[:, sl * DM:(sl + 1) * DM],
                           x_in[tt * 128:(tt + 1) * 128, :], "DS", f"ix{tt}")
                s.wait_tok("tensor", dtok, f"ix{tt}")
                for ht in range(8):
                    k = tt * 8 + ht
                    pb = k % 2
                    if k >= 2:
                        s.wait("tensor", "V", vhist[k - 2], f"ixp{k}")
                    pe = s.emit("tensor", lambda e, sl=sl, ht=ht, pb=pb: e.transpose(
                        psT0[:, pb * 512:pb * 512 + 128],
                        xin[:, sl * DM + ht * 128: sl * DM + ht * 128 + 128],
                        id_f[:]), inc="PE", tag=f"ixt{k}")
                    s.wait("vector", "PE", pe, f"ixv{k}")
                    v = s.emit("vector", lambda e, tt=tt, ht=ht, pb=pb: e.tensor_copy(
                        stream[:, ht * T + tt * 128: ht * T + tt * 128 + 128],
                        psT0[:, pb * 512:pb * 512 + 128]), inc="V", tag=f"ixv{k}")
                    vhist.append(v)
                pehist.append(s.counts["PE"])
            s.mark("x_loaded", "V")
        s.fence("postload")

        def emit_ln_chain(tag, half, B, dep_ms_list, router=None, dst_fn=None,
                          psc_v=None, psdt_free_ms=None, defer_norm=False, trunc=None):
            zbf, scr_t, psC = B["zbf"], B["scr_t"], B["psC"]
            psDt = B.get("psDt")
            t0h = half * HB
            router_orig = router
            do_comb = router is not None and trunc != "stats"
            lt_pe_hist = [None, None]
            if psc_v is not None:
                s.wait("tensor", "V", psc_v[0], f"{tag}_pscf")
            z2war = B.get("z2war")
            if z2war is not None:
                s.wait("scalar", "V", z2war[0], f"{tag}_z2war")
            for sub in range(8):
                t0 = t0h + sub * 128
                bank = sub % 2
                if sub >= 1:
                    s.wait_ms("scalar", f"{tag}_st{sub-1}", f"{tag}_zwar{sub}")
                for dep in dep_ms_list:
                    s.wait_ms("scalar", dep, f"{tag}_dep")
                for ht in range(8):
                    s.emit("scalar", lambda e, ht=ht, t0=t0: e.copy(
                        zbf[:, ht * 128: ht * 128 + 128],
                        stream[:, ht * T + t0: ht * T + t0 + 128]), inc="S", tag=f"{tag}_zc{sub}")
                s.mark(f"{tag}_zc{sub}", "S")
                s.wait_ms("tensor", f"{tag}_zc{sub}")
                if sub >= 2:
                    s.wait_ms("tensor", f"{tag}_rc{sub-2}", f"{tag}_bw{sub}")
                    if router is not None:
                        s.wait_ms("tensor", f"{tag}_lgv{sub-2}", f"{tag}_bw{sub}")
                for ht in range(8):
                    s.emit("tensor", lambda e, ht=ht, bank=bank: e.matmul(
                        psC[0:1, bank * 512: bank * 512 + 128],
                        statlhs[:, 0:1], zbf[:, ht * 128: ht * 128 + 128],
                        start=(ht == 0), stop=(ht == 7)),
                        inc=("PE" if ht == 7 else None), tag=f"{tag}_mu{sub}")
                if router is not None:
                    for ht in range(8):
                        s.emit("tensor", lambda e, ht=ht, bank=bank, l=router: e.matmul(
                            psC[0:4, bank * 512 + 256: bank * 512 + 384],
                            B["rwc"][:, (l * 8 + ht) * 4:(l * 8 + ht) * 4 + 4],
                            zbf[:, ht * 128: ht * 128 + 128],
                            start=(ht == 0), stop=(ht == 7)),
                            inc=("PE" if ht == 7 else None), tag=f"{tag}_rt{sub}")
                s.mark(f"{tag}_st1_{sub}", "PE")
                s.wait_ms("scalar", f"{tag}_st1_{sub}")
                for ht in range(8):
                    s.emit("scalar", lambda e, ht=ht, t0=t0: e.activation(
                        zbf[:, ht * 128: ht * 128 + 128],
                        stream[:, ht * T + t0: ht * T + t0 + 128], AF.Square),
                        inc="S", tag=f"{tag}_z2{sub}")
                s.mark(f"{tag}_zq{sub}", "S")
                s.wait_ms("tensor", f"{tag}_zq{sub}")
                for ht in range(8):
                    s.emit("tensor", lambda e, ht=ht, bank=bank: e.matmul(
                        psC[0:1, bank * 512 + 128: bank * 512 + 256],
                        statlhs[:, 1:2], zbf[:, ht * 128: ht * 128 + 128],
                        start=(ht == 0), stop=(ht == 7)),
                        inc=("PE" if ht == 7 else None), tag=f"{tag}_e2{sub}")
                s.mark(f"{tag}_st{sub}", "PE")
                s.wait_ms("scalar", f"{tag}_st{sub}")
                s.emit("scalar", lambda e, t0=t0, bank=bank: e.copy(
                    rows[0:1, t0: t0 + 128], psC[0:1, bank * 512: bank * 512 + 128]),
                    inc="S", tag=f"{tag}_rmu{sub}")
                s.emit("scalar", lambda e, t0=t0, bank=bank: e.copy(
                    rows[64:65, t0: t0 + 128], psC[0:1, bank * 512 + 128: bank * 512 + 256]),
                    inc="S", tag=f"{tag}_re2{sub}")
                s.mark(f"{tag}_rc{sub}", "S")
                if router is not None:
                    s.wait_ms("vector", f"{tag}_rc{sub}")
                    if lt_pe_hist[sub % 2] is not None:
                        s.wait("vector", "PE", lt_pe_hist[sub % 2], f"{tag}_lgw{sub}")
                    s.emit("vector", lambda e, sub=sub, bank=bank, l=router: e.tensor_scalar_add(
                        B["lgsb"][0:4, (sub % 2) * 128: (sub % 2) * 128 + 128],
                        psC[0:4, bank * 512 + 256: bank * 512 + 384],
                        B["rbc"][0:4, l:l + 1]), inc="V", tag=f"{tag}_lg{sub}")
                    s.mark(f"{tag}_lgv{sub}", "V")
                    if do_comb:
                        if sub == 0 and psdt_free_ms:
                            s.wait_ms("tensor", psdt_free_ms, f"{tag}_psdtf")
                        s.wait_ms("tensor", f"{tag}_lgv{sub}")
                        lt_pe_hist[sub % 2] = s.emit("tensor", lambda e, sub=sub: e.transpose(
                            psDt[:, sub * 4: sub * 4 + 4],
                            B["lgsb"][0:4, (sub % 2) * 128: (sub % 2) * 128 + 128],
                            id_b[0:4, 0:4]), inc="PE", tag=f"{tag}_lt{sub}")
                        if sub == 7:
                            s.mark(f"{tag}_lt", "PE")
            s.mark(f"{tag}_rcdone", "S")
            s.wait_ms("vector", f"{tag}_rcdone")
            s.emit("vector", lambda e: e.tensor_tensor(
                zbf[64:65, 0:HB], rows[0:1, t0h:t0h + HB],
                rows[0:1, t0h:t0h + HB], ALU.mult), inc="V", tag=f"{tag}_mu2")
            vv = s.emit("vector", lambda e: e.tensor_tensor(
                rows[32:33, t0h:t0h + HB], rows[64:65, t0h:t0h + HB],
                zbf[64:65, 0:HB], ALU.subtract), inc="V", tag=f"{tag}_var")
            if z2war is not None:
                z2war[0] = vv
            s.mark(f"{tag}_var", "V")
            s.wait_ms("scalar", f"{tag}_var")
            s.emit("scalar", lambda e: e.activation(
                rows[32:33, t0h:t0h + HB], rows[32:33, t0h:t0h + HB], AF.Ln, bias=EPS),
                inc="S", tag=f"{tag}_lnr")
            s.emit("scalar", lambda e: e.activation(
                rows[32:33, t0h:t0h + HB], rows[32:33, t0h:t0h + HB], AF.Exp, scale=-0.5),
                inc="S", tag=f"{tag}_expr")
            s.mark(f"{tag}_rows", "S")

            if router is not None and trunc == "stats":
                router = None
            if router is not None:
                lgsb, ltok, cwork, comb_rows = B["lgsb"], B["ltok"], B["cwork"], B["comb_rows"]
                s.wait_ms("vector", f"{tag}_lt")
                s.emit("vector", lambda e: e.tensor_copy(
                    ltok[:].rearrange("p (g t) -> p t g", g=4),
                    psDt[:, 0:32].rearrange("p (t g) -> p t g", g=4)), inc="V", tag=f"{tag}_ltc")
                s.emit("vector", lambda e: e.tensor_reduce(
                    cwork[:, 64:65], ltok[:], AX, ALU.max, negate=True), inc="V", tag=f"{tag}_max")
                s.mark(f"{tag}_max", "V")
                s.wait_ms("scalar", f"{tag}_max")
                s.emit("scalar", lambda e: e.activation(
                    cwork[:, 0:32], ltok[:], AF.Exp, bias=cwork[:, 64:65]),
                    inc="S", tag=f"{tag}_cexp")
                s.mark(f"{tag}_cexp", "S")
                s.wait_ms("vector", f"{tag}_cexp")
                if trunc == "exp":
                    s.emit("vector", lambda e: e.tensor_copy(
                        ltok[:], cwork[:, 0:32]), inc="V", tag=f"{tag}_tr_exp")

                def le(a):
                    return ltok[:, a * 8:(a + 1) * 8]

                def cw(a):
                    return cwork[:, 32 + a * 8: 32 + (a + 1) * 8]

                for a in (range(0) if trunc == "exp" else range(4)):
                    first = True
                    for b in range(4):
                        if a == b:
                            continue
                        s.emit("vector", lambda e, a=a, b=b: e.tensor_tensor(
                            cwork[:, 72:80], le(b), le(a), ALU.is_gt), inc="V", tag=f"{tag}_gt")
                        if first:
                            s.emit("vector", lambda e, a=a: e.tensor_copy(
                                cw(a), cwork[:, 72:80]), inc="V", tag=f"{tag}_cc")
                            first = False
                        else:
                            s.emit("vector", lambda e, a=a: e.tensor_tensor(
                                cw(a), cw(a), cwork[:, 72:80], ALU.add), inc="V", tag=f"{tag}_ca")
                for a in (range(0) if trunc == "exp" else range(4)):
                    s.emit("vector", lambda e, a=a: e.tensor_single_scalar(
                        cw(a), cw(a), 1.5, ALU.is_lt), inc="V", tag=f"{tag}_mk")
                    s.emit("vector", lambda e, a=a: e.tensor_tensor(
                        cw(a), cw(a), cwork[:, a * 8:(a + 1) * 8], ALU.mult), inc="V", tag=f"{tag}_wm")
                if trunc == "exp":
                    pass
                else:
                    s.emit("vector", lambda e: e.tensor_tensor(
                        cwork[:, 72:80], cw(0), cw(1), ALU.add), inc="V", tag=f"{tag}_d0")
                if trunc != "exp":
                    s.emit("vector", lambda e: e.tensor_tensor(
                        cwork[:, 72:80], cwork[:, 72:80], cw(2), ALU.add), inc="V", tag=f"{tag}_d1")
                    s.emit("vector", lambda e: e.tensor_tensor(
                        cwork[:, 72:80], cwork[:, 72:80], cw(3), ALU.add), inc="V", tag=f"{tag}_d2")
                    s.emit("vector", lambda e: e.reciprocal(
                        cwork[:, 80:88], cwork[:, 72:80]), inc="V", tag=f"{tag}_rcp")
                    for a in range(4):
                        s.emit("vector", lambda e, a=a: e.tensor_tensor(
                            le(a), cw(a), cwork[:, 80:88], ALU.mult), inc="V", tag=f"{tag}_cb")
                s.mark(f"{tag}_comb", "V")
                s.wait_ms("tensor", f"{tag}_comb")
                for j in range(8):
                    s.emit("tensor", lambda e, j=j: e.transpose(
                        psDt[0:4, j * 128: j * 128 + 128],
                        ltok[:].rearrange("p (g t) -> p t g", g=4)[:, j],
                        id_b[:]), inc=("PE" if j == 7 else None), tag=f"{tag}_ct{j}")
                s.mark(f"{tag}_ct", "PE")
                s.wait_ms("vector", f"{tag}_ct")
                s.emit("vector", lambda e, t0h=t0h: e.tensor_copy(
                    comb_rows[0:4, t0h:t0h + HB], psDt[0:4, 0:1024]), inc="V", tag=f"{tag}_crc")
                s.mark(f"{tag}_combr", "V")

            def emit_norm():
                if psc_v is not None:
                    s.wait("tensor", "V", psc_v[0], f"{tag}_pscn")
                for ch in range(2):
                    tc = t0h + ch * 512
                    s.wait_ms("tensor", f"{tag}_rows")
                    if ch >= 1:
                        s.wait_ms("tensor", f"{tag}_nrm0", f"{tag}_nw")
                    else:
                        s.wait_ms("tensor", f"{tag}_rc6", f"{tag}_nw")
                        s.wait_ms("tensor", f"{tag}_rc7", f"{tag}_nw")
                        if router_orig is not None:
                            s.wait_ms("tensor", f"{tag}_lgv6", f"{tag}_nw")
                            s.wait_ms("tensor", f"{tag}_lgv7", f"{tag}_nw")
                    s.emit("tensor", lambda e, tc=tc: e.matmul(
                        psC[:, 0:512], onesb[0:1, :], rows[0:1, tc:tc + 512],
                        start=True, stop=True), tag=f"{tag}_bmu{ch}")
                    s.emit("tensor", lambda e, tc=tc: e.matmul(
                        psC[:, 512:1024], onesb[32:33, :], rows[32:33, tc:tc + 512],
                        start=True, stop=True), inc="PE", tag=f"{tag}_brs{ch}")
                    s.mark(f"{tag}_bc{ch}", "PE")
                    s.wait_ms("vector", f"{tag}_bc{ch}")
                    for ht in range(8):
                        s.emit("vector", lambda e, ht=ht, tc=tc: e.tensor_tensor(
                            scr_t[:, 0:512], stream[:, ht * T + tc: ht * T + tc + 512],
                            psC[:, 0:512], ALU.add), inc="V", tag=f"{tag}_n1_{ch}_{ht}")
                        s.emit("vector", lambda e, o=dst_fn(ht, ch): e.tensor_tensor(
                            o, scr_t[:, 0:512], psC[:, 512:1024], ALU.mult),
                            inc="V", tag=f"{tag}_n2_{ch}_{ht}")
                    s.mark(f"{tag}_nrm{ch}", "V")
                s.mark(f"{tag}_done", "V")
                if psc_v is not None:
                    psc_v[0] = s.counts["V"]
            if defer_norm:
                return emit_norm
            emit_norm()

        def emit_ffn(call, pre_fence):
            with ExitStack() as sc:
                Fa = sc.enter_context
                zhat = Fa(nc.sbuf_tensor(f"zhat{call}", [128, 8 * HB], BF16))
                xe = Fa(nc.sbuf_tensor(f"xe{call}", [128, 8 * HB], BF16))
                hact = Fa(nc.sbuf_tensor(f"hact{call}", [128, 32 * HB], BF16))
                w1b = Fa(nc.sbuf_tensor(f"w1b{call}", [128, 3 * 2048], BF16))
                w2b = Fa(nc.sbuf_tensor(f"w2b{call}", [128, 3 * 2048], BF16))
                zbf = Fa(nc.sbuf_tensor(f"zbf{call}", [128, 8 * 128], BF16))
                scr_t = Fa(nc.sbuf_tensor(f"scrt{call}", [128, 512], F32))
                bcast = Fa(nc.sbuf_tensor(f"bcast{call}", [128, 2 * HB], BF16))
                comb_rows = Fa(nc.sbuf_tensor(f"combr{call}", [4, T], BF16))
                lgsb = Fa(nc.sbuf_tensor(f"lgsb{call}", [4, 256], BF16))
                ltok = Fa(nc.sbuf_tensor(f"ltok{call}", [128, 32], BF16))
                cwork = Fa(nc.sbuf_tensor(f"cwork{call}", [128, 96], F32))
                gcol = Fa(nc.sbuf_tensor(f"gcol{call}", [128, 128], F32))
                bcol = Fa(nc.sbuf_tensor(f"bcol{call}", [128, 128], F32))
                b1c = Fa(nc.sbuf_tensor(f"b1c{call}", [128, 512], F32))
                b2c = Fa(nc.sbuf_tensor(f"b2c{call}", [128, 128], F32))
                rwc = Fa(nc.sbuf_tensor(f"rwcs{call}", [128, 128], BF16))
                rbc = Fa(nc.sbuf_tensor(f"rbcs{call}", [4, 4], F32))
                psU = Fa(nc.psum_tensor(f"psU{call}", [128, 2560], F32))
                psC = Fa(nc.psum_tensor(f"psC{call}", [128, 1024], F32))
                psDt = Fa(nc.psum_tensor(f"psDt{call}", [128, 1024], BF16))

                B = {"zbf": zbf, "scr_t": scr_t, "psC": psC, "psDt": psDt,
                     "lgsb": lgsb, "ltok": ltok, "cwork": cwork, "comb_rows": comb_rows,
                     "rbc": rbc, "rwc": rwc, "z2war": [0]}

                s.wait_fence("gpsimd", pre_fence, f"f{call}")
                tok_gcol = dma("gpsimd", gcol[:], gcol_d, "DW", "aux")
                tok_bcol = dma("gpsimd", bcol[:], bcol_d, "DW", "aux")
                tok_b1c = dma("gpsimd", b1c[:], b1c_d, "DW", "aux")
                tok_b2c = dma("gpsimd", b2c[:], b2c_d, "DW", "aux")
                tok_rwc = dma("gpsimd", rwc[:], rwc_d, "DW", "aux")
                s.wait_fence("sync", pre_fence, f"f{call}")
                tok_rbc = dma("sync", rbc[:], rbc_d, "DS", "aux")
                tok_spill = dma("sync", xspill, stream[:], "DS", f"spill{call}")

                s.wait_tok("vector", tok_spill, "lnsp")
                s.wait_tok("vector", tok_rbc, "lnaux")
                s.wait_tok("vector", tok_gcol, "lnaux")
                s.wait_tok("vector", tok_bcol, "lnaux")
                s.wait_tok("vector", tok_b2c, "lnaux")
                s.wait_tok("scalar", tok_b1c, "lnaux")
                s.wait_tok("tensor", tok_rwc, "lnaux")

                nslots = depth * 2
                nunits = nslots * NE
                if stop == "ln":
                    nslots = 0
                    nunits = 0
                psc_v = [0]

                for half in range(2):
                    emit_ln_chain(
                        f"M{call}_{half}", half, B, [],
                        router=None,
                        dst_fn=(lambda ht, ch, half=half: stream[
                            :, ht * T + half * HB + ch * 512: ht * T + half * HB + ch * 512 + 512]),
                        psc_v=psc_v)

                chain_state = {"psdt": None}
                if stop == "ln":
                    s.mark(f"merge{call}", "V")
                    s.fence(f"ffn{call}_end")
                    return

                pending_norm = {}

                def emit_chain(sl):
                    l, half = divmod(sl, 2)
                    tagc = f"C{call}_{l}_{half}"
                    deps = [f"zupd{call}_{l-1}_{half}"] if l > 0 else [f"M{call}_{half}_done"]
                    pending_norm[sl] = emit_ln_chain(
                        tagc, half, B, deps, router=l,
                        dst_fn=(lambda ht, ch: zhat[
                            :, ht * HB + ch * 512: ht * HB + ch * 512 + 512]),
                        psc_v=psc_v,
                        psdt_free_ms=chain_state["psdt"], defer_norm=True,
                        trunc=chain_trunc)
                    if chain_trunc != "stats":
                        chain_state["psdt"] = f"{tagc}_combr"

                def emit_xe(sl, e, s1_war):
                    l, half = divmod(sl, 2)
                    s.wait_ms("vector", f"C{call}_{l}_{half}_done", f"xe{sl}_{e}")
                    if s1_war is not None:
                        s.wait("vector", s1_war[0], s1_war[1], f"xew{sl}_{e}")
                    col = (l * 4 + e) * 8
                    for ht in range(8):
                        s.emit("vector", lambda e_, ht=ht, col=col: e_.tensor_scalar(
                            xe[:, ht * HB: (ht + 1) * HB], zhat[:, ht * HB:(ht + 1) * HB],
                            gcol[:, col + ht: col + ht + 1], bcol[:, col + ht: col + ht + 1],
                            ALU.mult, ALU.add), inc="V", tag=f"xe{sl}_{e}_{ht}")
                    s.mark(f"xe{call}_{sl}_{e}", "V")

                bct_prev_v = [0]

                def emit_bcast(sl, e):
                    l, half = divmod(sl, 2)
                    par = e % 2
                    s.wait_ms("tensor", f"C{call}_{l}_{half}_combr", f"bc{sl}_{e}")
                    s.wait("tensor", "V", psc_v[0], f"bcw{sl}_{e}")
                    for ch in range(2):
                        s.emit("tensor", lambda e_, e=e, half=half, ch=ch: e_.matmul(
                            psC[:, ch * 512: ch * 512 + 512],
                            oh4[:, e * 128:(e + 1) * 128],
                            comb_rows[0:4, half * HB + ch * 512: half * HB + ch * 512 + 512],
                            start=True, stop=True),
                            inc=("PE" if ch == 1 else None), tag=f"bct{sl}_{e}")
                    s.mark(f"bct{call}_{sl}_{e}", "PE")
                    s.wait_ms("vector", f"bct{call}_{sl}_{e}")
                    psc_v[0] = s.emit("vector", lambda e_, par=par: e_.tensor_copy(
                        bcast[:, par * HB:(par + 1) * HB], psC[:, 0:1024]),
                        inc="V", tag=f"bcc{sl}_{e}")
                    s.mark(f"bcs{call}_{sl}_{e}", "V")

                emit_chain(0)
                pending_norm.pop(0)()
                emit_chain(1)
                if chain_trunc is None:
                    emit_xe(0, 0, None)
                    emit_bcast(0, 0)
                if stop == "chain":
                    s.mark(f"merge{call}", "V")
                    s.fence(f"ffn{call}_end")
                    return

                w1_dw, w2_dw = [], []
                evt_hist = []
                conv_list = []
                if call == 0:
                    for l2 in range(depth):
                        for e2 in range(NE):
                            conv_list.append((wt1_bf[l2, e2], wt[l2, e2, 0:16], (l2, e2, 1)))
                            conv_list.append((wt2_bf[l2, e2], wt[l2, e2, 16:32], (l2, e2, 2)))
                    asrc = {"wdq": wdq_d, "wuq": wuq_d, "wqr": wqr_d, "wdkv": wdkv_d,
                            "wuk": wuk_d, "wuv": wuv_d, "wkr": wkr_d, "wo": wo_d}
                    for nm in asrc:
                        conv_list.append((awq_bf[nm], asrc[nm], ("a", nm)))
                gel_base = s.counts["GEL"]
                evc_base = s.counts["EVC"]
                w1c_base = s.counts["W1C"]
                w2c_base = s.counts["W2C"]

                conv_toks = getattr(build_program, "_conv_toks", {})
                build_program._conv_toks = conv_toks
                for sl in range(nslots):
                    l, half = divmod(sl, 2)
                    for e in range(NE):
                        u = sl * NE + e
                        if call == 0:
                            for _ in range(6 if u < 2 else (2 if u < 20 else 8)):
                                if conv_list:
                                    dst, src_, key = conv_list.pop(0)
                                    conv_toks[key] = dma("gpsimd", dst, src_, "DW", f"cv{key}")
                        use_scr = (call == 1) or ((l, e, 1) in conv_toks and (l * 4 + e) >= 3)
                        for sb in range(16):
                            g = u * 16 + sb
                            if use_scr:
                                if g >= 3:
                                    s.wait("sync", "W1C", w1c_base + g - 2, f"w1f{g}")
                                if sb == 0:
                                    s.wait_tok("sync", conv_toks[(l, e, 1)], f"w1cv{u}")
                                w1_dw.append(dma(
                                    "sync", w1b[:, (g % 3) * 2048:(g % 3) * 2048 + 2048],
                                    wt1_bf[l, e, sb], "DS", f"w1d{g}"))
                            else:
                                if g >= 3:
                                    s.wait("gpsimd", "W1C", w1c_base + g - 2, f"w1f{g}")
                                w1_dw.append(dma(
                                    "gpsimd", w1b[:, (g % 3) * 2048:(g % 3) * 2048 + 2048],
                                    wt[l, e, sb], "DW", f"w1d{g}"))
                        for sb in range(16):
                            g = u * 16 + sb
                            if g >= 3:
                                s.wait("gpsimd", "W2C", w2c_base + g - 2, f"w2f{g}")
                            if use_scr:
                                if sb == 0:
                                    s.wait_tok("gpsimd", conv_toks[(l, e, 2)], f"w2cv{u}")
                                w2_dw.append(dma(
                                    "gpsimd", w2b[:, (g % 3) * 2048:(g % 3) * 2048 + 2048],
                                    wt2_bf[l, e, sb], "DW", f"w2d{g}"))
                            else:
                                w2_dw.append(dma(
                                    "gpsimd", w2b[:, (g % 3) * 2048:(g % 3) * 2048 + 2048],
                                    wt[l, e, 16 + sb], "DW", f"w2d{g}"))

                        s1_pair = []
                        pslots = []
                        s.wait_ms("tensor", f"xe{call}_{sl}_{e}", f"s1x{u}")
                        for ft in range(32):
                            if ft % 2 == 0:
                                s.wait_tok("tensor", w1_dw[u * 16 + ft // 2], f"s1w{u}_{ft}")
                            for c in range(2):
                                P = u * 64 + 2 * ft + c
                                evt = len(evt_hist)
                                slot = evt % 5
                                if evt >= 5:
                                    s.wait("tensor", evt_hist[evt - 5][0], evt_hist[evt - 5][1],
                                           f"s1g{u}_{ft}_{c}")
                                evt_hist.append(("GEL", gel_base + P + 1))
                                pslots.append(slot)
                                for ht in range(8):
                                    semn = None
                                    if ht == 7:
                                        semn = "W1C" if (c == 1 and ft % 2 == 1) else "PE"
                                    v = s.emit("tensor", lambda e_, g=(u * 16 + ft // 2), ht=ht, ft=ft, c=c, slot=slot: e_.matmul(
                                        psU[:, slot * 512: slot * 512 + 512],
                                        w1b[:, (g % 3) * 2048 + ht * 256 + (ft % 2) * 128:
                                            (g % 3) * 2048 + ht * 256 + (ft % 2) * 128 + 128],
                                        xe[:, ht * HB + c * 512: ht * HB + c * 512 + 512],
                                        start=(ht == 0), stop=(ht == 7)),
                                        inc=semn, tag=f"s1_{u}_{ft}_{c}")
                                    if ht == 7:
                                        s1_pair.append((semn, v))
                        s1_last = s1_pair[-1]
                        s.mark(f"s1end{call}_{u}", s1_last[0])

                        nxt = u + 1
                        if nxt < nunits:
                            nsl, ne_ = divmod(nxt, NE)
                            if ne_ == 0 and nsl in pending_norm:
                                pending_norm.pop(nsl)()
                            emit_xe(nsl, ne_, s1_last)
                            emit_bcast(nsl, ne_)
                        if u >= 1:
                            s.wait_ms("scalar", f"s2end{call}_{u-1}", f"gwar{u}")
                        for ft in range(32):
                            for c in range(2):
                                ptok = s1_pair[ft * 2 + c]
                                s.wait("scalar", ptok[0], ptok[1], f"g{u}_{ft}_{c}")
                                s.emit("scalar", lambda e_, ft=ft, c=c, l=l, e=e, slot=pslots[ft * 2 + c]: e_.activation(
                                    hact[:, ft * HB + c * 512: ft * HB + c * 512 + 512],
                                    psU[:, slot * 512: slot * 512 + 512], AF.Gelu,
                                    bias=b1c[:, (l * 4 + e) * 32 + ft:(l * 4 + e) * 32 + ft + 1]),
                                    inc="GEL", tag=f"g{u}_{ft}_{c}")

                        s2_tile = []
                        tslots = []
                        for ho in range(8):
                            for c in range(2):
                                q = u * 16 + ho * 2 + c
                                if c == 0:
                                    s.wait_tok("tensor", w2_dw[u * 16 + ho * 2], f"s2w{u}_{ho}")
                                evt = len(evt_hist)
                                slot = evt % 5
                                if evt >= 5:
                                    s.wait("tensor", evt_hist[evt - 5][0], evt_hist[evt - 5][1],
                                           f"s2e{u}_{ho}_{c}")
                                evt_hist.append(("EVC", evc_base + q + 1))
                                tslots.append(slot)
                                for ft in range(32):
                                    if ft == 16:
                                        s.wait_tok("tensor", w2_dw[u * 16 + ho * 2 + 1], f"s2w{u}_{ho}b")
                                    s.wait("tensor", "GEL",
                                           gel_base + u * 64 + ft * 2 + c + 1, f"s2g{u}")
                                    semn = None
                                    if ft == 31:
                                        semn = "W2C" if c == 1 else "PE"
                                    elif ft == 15 and c == 1:
                                        semn = "W2C"
                                    ni = 1
                                    v = s.emit("tensor", lambda e_, g=(u * 16 + ho * 2 + ft // 16), ft=ft, c=c, slot=slot: e_.matmul(
                                        psU[:, slot * 512: slot * 512 + 512],
                                        w2b[:, (g % 3) * 2048 + (ft % 16) * 128:
                                            (g % 3) * 2048 + (ft % 16) * 128 + 128],
                                        hact[:, ft * HB + c * 512: ft * HB + c * 512 + 512],
                                        start=(ft == 0), stop=(ft == 31)),
                                        inc=semn, inc_n=ni, tag=f"s2_{u}_{ho}_{c}")
                                    if ft == 31:
                                        s2_tile.append((semn, v))
                                    del v
                        s.mark(f"s2end{call}_{u}", "W2C")

                        s.wait_ms("vector", f"bcs{call}_{sl}_{e}", f"ev{u}")
                        for ho in range(8):
                            for c in range(2):
                                q = u * 16 + ho * 2 + c
                                stok = s2_tile[ho * 2 + c]
                                s.wait("vector", stok[0], stok[1], f"ev{u}_{ho}_{c}")
                                tmpv = scr_t[:].bitcast(BF16)
                                s.emit("vector", lambda e_, ho=ho, c=c, q=q, l=l, e=e, par=e % 2, tmpv=tmpv, slot=tslots[ho * 2 + c]: e_.scalar_tensor_tensor(
                                    tmpv[:, 0:512],
                                    psU[:, slot * 512: slot * 512 + 512],
                                    b2c[:, (l * 4 + e) * 8 + ho:(l * 4 + e) * 8 + ho + 1],
                                    bcast[:, par * HB + c * 512: par * HB + c * 512 + 512],
                                    ALU.add, ALU.mult), inc="EVC", tag=f"ev1_{u}_{ho}_{c}")
                                s.emit("vector", lambda e_, ho=ho, c=c, half=half, tmpv=tmpv: e_.tensor_tensor(
                                    stream[:, ho * T + half * HB + c * 512: ho * T + half * HB + c * 512 + 512],
                                    stream[:, ho * T + half * HB + c * 512: ho * T + half * HB + c * 512 + 512],
                                    tmpv[:, 0:512], ALU.add),
                                    inc="V", tag=f"ev2_{u}_{ho}_{c}")
                        if e == 0 and 2 <= sl + 1 < nslots:
                            emit_chain(sl + 1)
                        if stop == f"u{u}":
                            s.mark(f"merge{call}", "V")
                            s.fence(f"ffn{call}_end")
                            return
                    s.mark(f"zupd{call}_{l}_{half}", "V")

                if dbg and call == dbg_call:
                    s.wait_ms("gpsimd", f"zupd{call}_{nslots//2-1}_{1}", "dbg")
                    dma("gpsimd", dbg_comb, comb_rows[:], "DW", "dbg")
                    dma("gpsimd", dbg_zhat, zhat[:], "DW", "dbg")
                    dma("gpsimd", dbg_xe, xe[:], "DW", "dbg")
                    dma("gpsimd", dbg_z, stream[:], "DW", "dbg")
                    s.fence("dbg_done")

                if dbg and call == dbg_call:
                    s.wait_fence("vector", "dbg_done", "dbgmg")
                xld = hact[:].bitcast(F32)
                s.wait_ms("sync", f"s2end{call}_{nunits-1}", "mgw")
                mg_tok = dma("sync", xld, xspill, "DS", f"mgld{call}")
                s.wait_tok("vector", mg_tok, "mg")
                s.wait_ms("vector", f"zupd{call}_{nslots//2 - 1}_{1}", "mgz")
                for ho in range(8):
                    for c in range(4):
                        s.emit("vector", lambda e_, ho=ho, c=c: e_.scalar_tensor_tensor(
                            stream[:, ho * T + c * 512: ho * T + c * 512 + 512],
                            stream[:, ho * T + c * 512: ho * T + c * 512 + 512], 0.5,
                            xld[:, ho * T + c * 512: ho * T + c * 512 + 512],
                            ALU.mult, ALU.add), inc="V", tag=f"mg{call}_{ho}_{c}")
                s.mark(f"merge{call}", "V")
            s.fence(f"ffn{call}_end")

        def emit_attention(pre_fence):
            stream_bf = stream[:].bitcast(BF16)

            def kc_sl(hslice, c0, w):
                pb, pw = hslice
                return stream[pb:pb + pw, c0 // 2: (c0 + w) // 2].bitcast(BF16)

            def vt_sl(c0, w):
                return stream[:, (16384 + c0) // 2: (16384 + c0 + w) // 2].bitcast(BF16)

            with ExitStack() as sc:
                Aa = sc.enter_context
                qC = Aa(nc.sbuf_tensor("qC", [128, 8 * T], BF16))
                qR = Aa(nc.sbuf_tensor("qR", [96, 6 * T], BF16))
                kR3 = Aa(nc.sbuf_tensor("kR3", [96, T], BF16))
                psP = Aa(nc.psum_tensor("psP", [128, 1024], F32))
                psT1 = Aa(nc.psum_tensor("psT1", [128, 1024], BF16))
                psT2 = Aa(nc.psum_tensor("psT2", [128, 1024], BF16))

                s.wait_fence("sync", pre_fence, "attn")
                tok_aspill = dma("sync", xspill, stream[:], "DS", "aspill")

                with ExitStack() as scA:
                    Ab = scA.enter_context
                    cKV = Ab(nc.sbuf_tensor("cKV", [128, 2 * T], BF16))
                    wdkvb = Ab(nc.sbuf_tensor("wdkvb", [128, 2 * 1024], BF16))
                    wukb = Ab(nc.sbuf_tensor("wukb", [128, 2048], BF16))
                    wuvb = Ab(nc.sbuf_tensor("wuvb", [128, 2048], BF16))
                    psp_user_v = [0, 0]
                    with ExitStack() as scB:
                        Ac = scB.enter_context
                        xhat = Ac(nc.sbuf_tensor("xhat", [128, 8 * HB], BF16))
                        cq8 = Ac(nc.sbuf_tensor("cq8", [128, 32 * 256], BF16))
                        wdqb = Ac(nc.sbuf_tensor("wdqb", [128, 2 * 1024], BF16))
                        wuqb = Ac(nc.sbuf_tensor("wuqb", [128, 2 * 1024], BF16))
                        wqrb = Ac(nc.sbuf_tensor("wqrb", [128, 2 * 768], BF16))
                        wkrb = Ac(nc.sbuf_tensor("wkrb", [128, 256], BF16))
                        qrp = Ac(nc.sbuf_tensor("qrp", [96, T], BF16))
                        azbf = Ac(nc.sbuf_tensor("azbf", [128, 8 * 128], BF16))
                        ascr = Ac(nc.sbuf_tensor("ascr", [128, 512], F32))
                        psCa = Ac(nc.psum_tensor("psCa", [128, 1024], F32))

                        Bа = {"zbf": azbf, "scr_t": ascr, "psC": psCa,
                              "z2war": [0]}

                        s.wait_fence("gpsimd", pre_fence, "attnw")
                        ctk = build_program._conv_toks
                        for nm in ("wdkv", "wuk", "wuv", "wkr", "wdq", "wuq", "wqr", "wo"):
                            s.wait_tok("sync", ctk[("a", nm)], "aconv")
                        res_toks = [
                            dma("sync", wdkvb[:, 0:1024], awq_bf["wdkv"][0], "DS", "adkv"),
                            dma("sync", wdkvb[:, 1024:2048], awq_bf["wdkv"][1], "DS", "adkv"),
                            dma("sync", wukb[:], awq_bf["wuk"], "DS", "auk"),
                            dma("sync", wuvb[:], awq_bf["wuv"], "DS", "auv"),
                            dma("sync", wkrb[:], awq_bf["wkr"], "DS", "akr"),
                        ]
                        tok_qrp = dma("sync", qrp[:], cst["qrope"].ap(), "DS", "aqrp")
                        s.wait_tok("vector", tok_qrp, "aqrp")
                        for tk in res_toks:
                            s.wait_tok("tensor", tk, "ares")

                        wdq_hist = []
                        wdq_pe = []
                        wuq_hist, wuq_pe = [], []
                        wqr_hist, wqr_pe = [], []

                        apsc_v = [0]
                        for half in range(2):
                            emit_ln_chain(
                                f"A{half}", half, Bа, [f"merge0"] if half == 0 else [],
                                router=None,
                                dst_fn=(lambda ht, ch: xhat[
                                    :, ht * HB + ch * 512: ht * HB + ch * 512 + 512]),
                                psc_v=apsc_v)
                            t0h = half * HB

                            for kt in range(2):
                                for ch in range(2):
                                    slot = (kt * 2 + ch) % 2
                                    s.wait("tensor", "V", psp_user_v[slot], f"ckv{kt}{ch}")
                                    s.wait_ms("tensor", f"A{half}_done", "ckvx")
                                    for ht in range(8):
                                        s.emit("tensor", lambda e, kt=kt, ch=ch, ht=ht, slot=slot, t0h=t0h: e.matmul(
                                            psP[:, slot * 512: slot * 512 + 512],
                                            wdkvb[:, kt * 1024 + ht * 128: kt * 1024 + ht * 128 + 128],
                                            xhat[:, ht * HB + ch * 512: ht * HB + ch * 512 + 512],
                                            start=(ht == 0), stop=(ht == 7)),
                                            inc=("PE" if ht == 7 else None), tag=f"ckv{kt}{ch}")
                                    pe = s.counts["PE"]
                                    s.wait("vector", "PE", pe, f"ckve{kt}{ch}")
                                    psp_user_v[slot] = s.emit("vector", lambda e, kt=kt, ch=ch, slot=slot, t0h=t0h: e.tensor_copy(
                                        cKV[:, kt * T + t0h + ch * 512: kt * T + t0h + ch * 512 + 512],
                                        psP[:, slot * 512: slot * 512 + 512]), inc="V", tag=f"ckve{kt}{ch}")
                            for ch in range(2):
                                slot = ch % 2
                                s.wait("tensor", "V", psp_user_v[slot], f"kr{ch}")
                                for ht in range(8):
                                    s.emit("tensor", lambda e, ch=ch, ht=ht, slot=slot, t0h=t0h: e.matmul(
                                        psP[0:32, slot * 512: slot * 512 + 512],
                                        wkrb[:, ht * 32: ht * 32 + 32],
                                        xhat[:, ht * HB + ch * 512: ht * HB + ch * 512 + 512],
                                        start=(ht == 0), stop=(ht == 7)),
                                        inc=("PE" if ht == 7 else None), tag=f"kr{ch}")
                                pe = s.counts["PE"]
                                s.wait("vector", "PE", pe, f"kre{ch}")
                                for bb in range(3):
                                    psp_user_v[slot] = s.emit("vector", lambda e, ch=ch, slot=slot, bb=bb, t0h=t0h: e.tensor_tensor(
                                        kR3[bb * 32:(bb + 1) * 32, t0h + ch * 512: t0h + ch * 512 + 512],
                                        psP[0:32, slot * 512: slot * 512 + 512],
                                        qrp[0:32, t0h + ch * 512: t0h + ch * 512 + 512],
                                        ALU.mult), inc="V", tag=f"kre{ch}_{bb}")

                            for q8 in range(4):
                                tw = t0h + q8 * 256
                                lw = q8 * 256
                                for dqt in range(32):
                                    gq = (half * 4 + q8) * 32 + dqt
                                    if gq >= 2:
                                        s.wait("sync", "PE", wdq_pe[gq - 2], f"wdqf{gq}")
                                    wdq_hist.append(dma(
                                        "sync", wdqb[:, (gq % 2) * 1024:(gq % 2) * 1024 + 1024],
                                        awq_bf["wdq"][dqt], "DS", f"wdq{gq}"))
                                    slot = dqt % 2
                                    s.wait_tok("tensor", wdq_hist[gq], f"cq{gq}")
                                    s.wait("tensor", "V", psp_user_v[slot], f"cq{gq}")
                                    for ht in range(8):
                                        s.emit("tensor", lambda e, gq=gq, ht=ht, lw=lw, slot=slot: e.matmul(
                                            psP[:, slot * 512: slot * 512 + 256],
                                            wdqb[:, (gq % 2) * 1024 + ht * 128:(gq % 2) * 1024 + ht * 128 + 128],
                                            xhat[:, ht * HB + lw: ht * HB + lw + 256],
                                            start=(ht == 0), stop=(ht == 7)),
                                            inc=("PE" if ht == 7 else None), tag=f"cq{gq}")
                                    wdq_pe.append(s.counts["PE"])
                                    s.wait("vector", "PE", s.counts["PE"], f"cqe{gq}")
                                    psp_user_v[slot] = s.emit("vector", lambda e, dqt=dqt, slot=slot: e.tensor_copy(
                                        cq8[:, dqt * 256: dqt * 256 + 256],
                                        psP[:, slot * 512: slot * 512 + 256]), inc="V", tag=f"cqe{gq}")
                                cq_ready_v = s.counts["V"]
                                for ho in range(8):
                                    gq = (half * 4 + q8) * 8 + ho
                                    slot = ho % 2
                                    s.wait("tensor", "V", cq_ready_v, f"uq{gq}")
                                    s.wait("tensor", "V", psp_user_v[slot], f"uq{gq}")
                                    for sb in range(4):
                                        gg = gq * 4 + sb
                                        if gg >= 2:
                                            s.wait("sync", "PE", wuq_pe[gg - 2], f"wuqf{gg}")
                                        wuq_hist.append(dma(
                                            "sync", wuqb[:, (gg % 2) * 1024:(gg % 2) * 1024 + 1024],
                                            awq_bf["wuq"][ho * 4 + sb], "DS", f"wuq{gg}"))
                                        s.wait_tok("tensor", wuq_hist[gg], f"uq{gg}")
                                        for dq in range(8):
                                            s.emit("tensor", lambda e, gg=gg, sb=sb, dq=dq, slot=slot: e.matmul(
                                                psP[:, slot * 512: slot * 512 + 256],
                                                wuqb[:, (gg % 2) * 1024 + dq * 128:(gg % 2) * 1024 + dq * 128 + 128],
                                                cq8[:, (sb * 8 + dq) * 256: (sb * 8 + dq) * 256 + 256],
                                                start=(sb == 0 and dq == 0), stop=(sb == 3 and dq == 7)),
                                                inc=("PE" if dq == 7 else None), tag=f"uq{gg}")
                                        wuq_pe.append(s.counts["PE"])
                                    s.wait("vector", "PE", s.counts["PE"], f"uqe{gq}")
                                    psp_user_v[slot] = s.emit("vector", lambda e, ho=ho, tw=tw, slot=slot: e.tensor_copy(
                                        qC[:, ho * T + tw: ho * T + tw + 256],
                                        psP[:, slot * 512: slot * 512 + 256]), inc="V", tag=f"uqe{gq}")
                                for gr in range(6):
                                    gq = (half * 4 + q8) * 6 + gr
                                    slot = gr % 2
                                    s.wait("tensor", "V", psp_user_v[slot], f"qr{gq}")
                                    for sb in range(4):
                                        gg = gq * 4 + sb
                                        if gg >= 2:
                                            s.wait("sync", "PE", wqr_pe[gg - 2], f"wqrf{gg}")
                                        wqr_hist.append(dma(
                                            "sync", wqrb[:, (gg % 2) * 768:(gg % 2) * 768 + 768],
                                            awq_bf["wqr"][gr * 4 + sb], "DS", f"wqr{gg}"))
                                        s.wait_tok("tensor", wqr_hist[gg], f"qr{gg}")
                                        for dq in range(8):
                                            s.emit("tensor", lambda e, gg=gg, sb=sb, dq=dq, slot=slot: e.matmul(
                                                psP[0:96, slot * 512: slot * 512 + 256],
                                                wqrb[:, (gg % 2) * 768 + dq * 96:(gg % 2) * 768 + dq * 96 + 96],
                                                cq8[:, (sb * 8 + dq) * 256: (sb * 8 + dq) * 256 + 256],
                                                start=(sb == 0 and dq == 0), stop=(sb == 3 and dq == 7)),
                                                inc=("PE" if dq == 7 else None), tag=f"qr{gg}")
                                        wqr_pe.append(s.counts["PE"])
                                    s.wait("vector", "PE", s.counts["PE"], f"qre{gq}")
                                    psp_user_v[slot] = s.emit("vector", lambda e, gr=gr, tw=tw, slot=slot: e.tensor_tensor(
                                        qR[0:96, gr * T + tw: gr * T + tw + 256],
                                        psP[0:96, slot * 512: slot * 512 + 256],
                                        qrp[0:96, tw: tw + 256], ALU.mult), inc="V", tag=f"qre{gq}")
                        s.mark("proj_done_pe", "PE")
                        s.mark("proj_done_v", "V")

                    with ExitStack() as scB2:
                        vtmp = scB2.enter_context(nc.sbuf_tensor("vtmp", [128, 512], BF16))
                        s.wait_tok("vector", tok_aspill, "kv")
                        pst_user_v = [0, 0]
                        psp2_user_v = psp_user_v
                        for ho in range(8):
                            for ch in range(4):
                                slot = ch % 2
                                s.wait("tensor", "V", psp2_user_v[slot], f"kc{ho}{ch}")
                                for kt in range(2):
                                    s.emit("tensor", lambda e, ho=ho, ch=ch, kt=kt, slot=slot: e.matmul(
                                        psP[:, slot * 512: slot * 512 + 512],
                                        wukb[:, kt * 1024 + ho * 128: kt * 1024 + ho * 128 + 128],
                                        cKV[:, kt * T + ch * 512: kt * T + ch * 512 + 512],
                                        start=(kt == 0), stop=(kt == 1)),
                                        inc=("PE" if kt == 1 else None), tag=f"kc{ho}{ch}")
                                s.wait("vector", "PE", s.counts["PE"], f"kce{ho}{ch}")
                                psp2_user_v[slot] = s.emit("vector", lambda e, ho=ho, ch=ch, slot=slot: e.tensor_copy(
                                    kc_sl((0, 128), ho * T + ch * 512, 512),
                                    psP[:, slot * 512: slot * 512 + 512]), inc="V", tag=f"kce{ho}{ch}")
                                slot2 = (ch + 1) % 2
                                s.wait("tensor", "V", psp2_user_v[slot2], f"v{ho}{ch}")
                                for kt in range(2):
                                    s.emit("tensor", lambda e, ho=ho, ch=ch, kt=kt, slot2=slot2: e.matmul(
                                        psP[:, slot2 * 512: slot2 * 512 + 512],
                                        wuvb[:, kt * 1024 + ho * 128: kt * 1024 + ho * 128 + 128],
                                        cKV[:, kt * T + ch * 512: kt * T + ch * 512 + 512],
                                        start=(kt == 0), stop=(kt == 1)),
                                        inc=("PE" if kt == 1 else None), tag=f"v{ho}{ch}")
                                s.wait("vector", "PE", s.counts["PE"], f"ve{ho}{ch}")
                                vv = s.emit("vector", lambda e, slot2=slot2: e.tensor_copy(
                                    vtmp[:], psP[:, slot2 * 512: slot2 * 512 + 512]), inc="V", tag=f"ve{ho}{ch}")
                                psp2_user_v[slot2] = vv
                                for j in range(4):
                                    tslot = j % 2
                                    pst = psT1 if tslot == 0 else psT2
                                    s.wait("tensor", "V", vv, f"vt{ho}{ch}{j}")
                                    s.wait("tensor", "V", pst_user_v[tslot], f"vt{ho}{ch}{j}")
                                    s.emit("tensor", lambda e, j=j, pst=pst: e.transpose(
                                        pst[:, 0:128], vtmp[:, j * 128: j * 128 + 128],
                                        id_b[:]), inc="PE", tag=f"vt{ho}{ch}{j}")
                                    s.wait("vector", "PE", s.counts["PE"], f"vte{ho}{ch}{j}")
                                    pst_user_v[tslot] = s.emit("vector", lambda e, ho=ho, ch=ch, j=j, pst=pst: e.tensor_copy(
                                        vt_sl((ch * 4 + j) * 1024 + ho * 128, 128),
                                        pst[:, 0:128]), inc="V", tag=f"vte{ho}{ch}{j}")
                        s.mark("kv_done_pe", "PE")
                        s.mark("kv_done_v", "V")

                with ExitStack() as scC:
                    Ad = scC.enter_context
                    yall = Ad(nc.sbuf_tensor("yall", [128, 16 * 1024], BF16))
                    srow = Ad(nc.sbuf_tensor("srow", [128, 2 * 2048], F32))
                    prow = Ad(nc.sbuf_tensor("prow", [128, 2 * 2048], BF16))
                    pTs = Ad(nc.sbuf_tensor("pTs", [128, 2 * 2048], BF16))
                    psS = Ad(nc.psum_tensor("psS", [128, 1024], F32))
                    psY1 = Ad(nc.psum_tensor("psY1", [128, 512], F32))
                    psY2 = Ad(nc.psum_tensor("psY2", [128, 512], F32))

                    s.wait_ms("tensor", "kv_done_v", "sc0")
                    pss_user = [("V", 0), ("V", 0)]
                    pst_user_v = [0, 0]
                    it_hist = []

                    def stage_a(it, h, qt):
                        sl_ = it % 2
                        hpb = (h % 2) * 64
                        hpt = h // 2
                        qgrp = h // 3
                        qoff = 32 * (h % 3)
                        L = (qt + 1) * 128
                        nfull = L // 512
                        chunks = [(j * 512, 512) for j in range(nfull)]
                        if L % 512:
                            chunks.append((nfull * 512, L % 512))
                        rec = {"sl": sl_, "L": L}
                        for j, (c0, w) in enumerate(chunks):
                            slot = j % 2
                            s.wait("tensor", pss_user[slot][0], pss_user[slot][1], f"sc{it}_{j}")
                            s.emit("tensor", lambda e, c0=c0, w=w, slot=slot, hpb=hpb, hpt=hpt, qt=qt: e.matmul(
                                psS[:, slot * 512: slot * 512 + w],
                                qC[hpb:hpb + 64, hpt * T + qt * 128: hpt * T + qt * 128 + 128],
                                kc_sl((hpb, 64), hpt * T + c0, w),
                                start=True, stop=False), tag=f"sc{it}_{j}")
                            pe = s.emit("tensor", lambda e, c0=c0, w=w, slot=slot, qoff=qoff, qgrp=qgrp, qt=qt: e.matmul(
                                psS[:, slot * 512: slot * 512 + w],
                                qR[qoff:qoff + 32, qgrp * T + qt * 128: qgrp * T + qt * 128 + 128],
                                kR3[qoff:qoff + 32, c0: c0 + w],
                                start=False, stop=True), inc="PE", tag=f"sc{it}_{j}b")
                            last = (j == len(chunks) - 1)
                            wcopy = w - 128 if last else w
                            s.wait("scalar", "PE", pe, f"sce{it}_{j}")
                            if wcopy > 0:
                                sc = s.emit("scalar", lambda e, c0=c0, w=wcopy, slot=slot, sl_=sl_: e.copy(
                                    srow[:, sl_ * 2048 + c0: sl_ * 2048 + c0 + w],
                                    psS[:, slot * 512: slot * 512 + w]), inc="S", tag=f"sce{it}_{j}")
                                pss_user[slot] = ("S", sc)
                            if last:
                                s.wait("vector", "PE", pe, f"scm{it}")
                                s.wait("vector", "S", s.counts["S"], f"scms{it}")
                                if it >= 2:
                                    s.wait("vector", "S", it_hist[it - 2]["exp_s"], f"scmw{it}")
                                vm = s.emit("vector", lambda e, c0=c0, w=w, slot=slot, sl_=sl_: e.tensor_tensor(
                                    srow[:, sl_ * 2048 + c0 + w - 128: sl_ * 2048 + c0 + w],
                                    psS[:, slot * 512 + w - 128: slot * 512 + w],
                                    tri[:], ALU.add), inc="V", tag=f"scm{it}")
                                pss_user[slot] = ("V", vm)
                        copies_s = s.counts["S"]
                        s.wait("vector", "S", copies_s, f"mx{it}")
                        s.emit("vector", lambda e, sl_=sl_, L=L: e.tensor_reduce(
                            smalls[:, 6 + sl_:7 + sl_], srow[:, sl_ * 2048: sl_ * 2048 + L],
                            AX, ALU.max, negate=True), inc="V", tag=f"mx{it}")
                        rec["mxs_v"] = s.emit("vector", lambda e, sl_=sl_: e.tensor_single_scalar(
                            smalls[:, 0 + sl_:1 + sl_], smalls[:, 6 + sl_:7 + sl_],
                            ISQ, ALU.mult), inc="V", tag=f"mxs{it}")
                        return rec

                    def stage_b(it, h, qt, rec):
                        sl_ = rec["sl"]
                        L = rec["L"]
                        s.wait("scalar", "V", rec["mxs_v"], f"ex{it}")
                        if it >= 2:
                            s.wait("scalar", "PE", it_hist[it - 2]["pt_pe"], f"pw{it}")
                        exp_s = s.emit("scalar", lambda e, sl_=sl_, L=L: e.activation(
                            prow[:, sl_ * 2048: sl_ * 2048 + L],
                            srow[:, sl_ * 2048: sl_ * 2048 + L], AF.Exp,
                            bias=smalls[:, 0 + sl_:1 + sl_], scale=ISQ,
                            accum_out=smalls[:, 2 + sl_:3 + sl_]), inc="S", tag=f"ex{it}")
                        s.wait("vector", "S", exp_s, f"rc{it}")
                        s.emit("vector", lambda e, sl_=sl_: e.reciprocal(
                            smalls[:, 4 + sl_:5 + sl_], smalls[:, 2 + sl_:3 + sl_]),
                            inc="V", tag=f"rc{it}")
                        psY = psY1 if sl_ == 0 else psY2
                        nkt = qt + 1
                        grpv = []
                        s.wait("tensor", "S", exp_s, f"pt{it}")
                        for g8 in range((nkt + 7) // 8):
                            k0 = g8 * 8
                            k1 = min(k0 + 8, nkt)
                            tslot = g8 % 2
                            pst = psT1 if tslot == 0 else psT2
                            s.wait("tensor", "V", pst_user_v[tslot], f"pt{it}_{g8}")
                            pe = None
                            for kt in range(k0, k1):
                                pe = s.emit("tensor", lambda e, kt=kt, k0=k0, sl_=sl_, pst=pst: e.transpose(
                                    pst[:, (kt - k0) * 128: (kt - k0) * 128 + 128],
                                    prow[:, sl_ * 2048 + kt * 128: sl_ * 2048 + kt * 128 + 128],
                                    id_b[:]), inc=("PE" if kt == k1 - 1 else None), tag=f"pt{it}_{kt}")
                            s.wait("vector", "PE", pe, f"pte{it}_{g8}")
                            pst_user_v[tslot] = s.emit("vector", lambda e, k0=k0, k1=k1, sl_=sl_, pst=pst: e.tensor_copy(
                                pTs[:, sl_ * 2048 + k0 * 128: sl_ * 2048 + k1 * 128],
                                pst[:, 0:(k1 - k0) * 128]), inc="V", tag=f"pte{it}_{g8}")
                            grpv.append((k1, s.counts["V"]))
                        pt_pe = s.counts["PE"]
                        if it >= 2:
                            s.wait("tensor", "V", it_hist[it - 2]["yev_v"], f"yw{it}")
                        y_pe = None
                        gi = 0
                        for kt in range(nkt):
                            if kt >= grpv[gi][0]:
                                gi += 1
                            s.wait("tensor", "V", grpv[gi][1], f"y{it}_{kt}")
                            y_pe = s.emit("tensor", lambda e, kt=kt, sl_=sl_, h=h, psY=psY, nkt=nkt: e.matmul(
                                psY[:, 0:64],
                                pTs[:, sl_ * 2048 + kt * 128: sl_ * 2048 + kt * 128 + 128],
                                vt_sl(kt * 1024 + h * 64, 64),
                                start=(kt == 0), stop=(kt == nkt - 1)),
                                inc=("PE" if kt == nkt - 1 else None), tag=f"y{it}_{kt}")
                        s.wait("vector", "PE", y_pe, f"ye{it}")
                        yev_v = s.emit("vector", lambda e, qt=qt, h=h, sl_=sl_, psY=psY: e.tensor_scalar_mul(
                            yall[:, qt * 1024 + h * 64: qt * 1024 + h * 64 + 64],
                            psY[:, 0:64], smalls[:, 4 + sl_:5 + sl_]), inc="V", tag=f"ye{it}")
                        it_hist.append({"exp_s": exp_s, "pt_pe": pt_pe, "y_pe": y_pe,
                                        "yev_v": yev_v})

                    iters = [(h, qt) for h in range(NH) for qt in range(16)]
                    recs = {}
                    for it, (h, qt) in enumerate(iters):
                        recs[it] = stage_a(it, h, qt)
                        if it >= 1:
                            ph, pqt = iters[it - 1]
                            stage_b(it - 1, ph, pqt, recs.pop(it - 1))
                    stage_b(len(iters) - 1, *iters[-1], recs.pop(len(iters) - 1))
                    s.mark("scores_done_pe", "PE")
                    s.mark("scores_done_v", "V")

                    yfm = qC
                    s.wait_ms("vector", "scores_done_pe", "yfm")
                    pst_user_v2 = [0, 0]
                    for qt in range(16):
                        for dt in range(8):
                            k = qt * 8 + dt
                            tslot = k % 2
                            pst = psT1 if tslot == 0 else psT2
                            s.wait("tensor", "V", it_hist[-1]["yev_v"], "yfm0")
                            s.wait("tensor", "V", pst_user_v[tslot], f"yf{k}")
                            s.wait("tensor", "V", pst_user_v2[tslot], f"yf{k}")
                            pe = s.emit("tensor", lambda e, qt=qt, dt=dt, pst=pst: e.transpose(
                                pst[:, 0:128], yall[:, qt * 1024 + dt * 128: qt * 1024 + dt * 128 + 128],
                                id_b[:]), inc="PE", tag=f"yf{k}")
                            s.wait("vector", "PE", pe, f"yfe{k}")
                            pst_user_v2[tslot] = s.emit("vector", lambda e, qt=qt, dt=dt, pst=pst: e.tensor_copy(
                                yfm[:, dt * T + qt * 128: dt * T + qt * 128 + 128],
                                pst[:, 0:128]), inc="V", tag=f"yfe{k}")
                    yfm_v = s.counts["V"]

                    wob = prow
                    xld2 = srow
                    wo_pe = []
                    psp_user_v3 = [0, 0]
                    s.wait("gpsimd", "S", it_hist[-1]["exp_s"], "wo0")
                    s.wait("gpsimd", "V", yfm_v, "wo0")
                    s.wait("sync", "S", it_hist[-1]["exp_s"], "wo0")
                    s.wait("sync", "V", yfm_v, "wo0")
                    mg_v = []
                    for ho in range(8):
                        if ho >= 2:
                            s.wait("gpsimd", "PE", wo_pe[ho - 2], f"wof{ho}")
                        wo_tok = dma("gpsimd", wob[:, (ho % 2) * 1024:(ho % 2) * 1024 + 1024],
                                     awq_bf["wo"][ho], "DW", f"wo{ho}")
                        if ho >= 2:
                            s.wait("sync", "V", mg_v[ho - 2], f"xlf{ho}")
                        xld_tok = dma("sync", xld2[:, (ho % 2) * 2048:(ho % 2) * 2048 + 2048],
                                      xspill[:, ho * T:(ho + 1) * T], "DS", f"xl{ho}")
                        s.wait_tok("tensor", wo_tok, f"wo{ho}")
                        s.wait("tensor", "V", yfm_v, "wo")
                        for c in range(4):
                            slot = c % 2
                            s.wait("tensor", "V", psp_user_v3[slot], f"woc{ho}{c}")
                            for dt in range(8):
                                s.emit("tensor", lambda e, ho=ho, c=c, dt=dt, slot=slot: e.matmul(
                                    psP[:, slot * 512: slot * 512 + 512],
                                    wob[:, (ho % 2) * 1024 + dt * 128:(ho % 2) * 1024 + dt * 128 + 128],
                                    yfm[:, dt * T + c * 512: dt * T + c * 512 + 512],
                                    start=(dt == 0), stop=(dt == 7)),
                                    inc=("PE" if dt == 7 else None), tag=f"wo{ho}{c}")
                            pe = s.counts["PE"]
                            s.wait("vector", "PE", pe, f"woe{ho}{c}")
                            s.wait_tok("vector", xld_tok, f"woe{ho}{c}")
                            v = s.emit("vector", lambda e, ho=ho, c=c, slot=slot: e.scalar_tensor_tensor(
                                stream[:, ho * T + c * 512: ho * T + c * 512 + 512],
                                psP[:, slot * 512: slot * 512 + 512], 1.0,
                                xld2[:, (ho % 2) * 2048 + c * 512:(ho % 2) * 2048 + c * 512 + 512],
                                ALU.mult, ALU.add), inc="V", tag=f"woe{ho}{c}")
                            psp_user_v3[slot] = v
                        wo_pe.append(s.counts["PE"])
                        mg_v.append(s.counts["V"])
                    s.mark("merge_attn", "V")
            s.fence("attn_end")

        emit_ffn(0, "postload")
        if do_attn:
            emit_attention("ffn0_end")
            if do_ffn2:
                emit_ffn(1, "attn_end")
                final_ms = "merge1"
            else:
                final_ms = "merge_attn"
        else:
            final_ms = "merge0"

        with ExitStack() as sco:
            obuf = sco.enter_context(nc.sbuf_tensor("obuf", [128, 2 * DM], F32))
            psTo = sco.enter_context(nc.psum_tensor("psTo", [128, 1024], F32))
            s.wait_ms("tensor", final_ms, "out")
            if dbg:
                s.wait_fence("vector", "dbg_done", "out")
                s.wait_fence("tensor", "dbg_done", "out")
            ds_hist = []
            ob_v = []
            for tt in range(16):
                sl = tt % 2
                for ht in range(8):
                    k = tt * 8 + ht
                    pb = k % 2
                    if k >= 2:
                        s.wait("tensor", "V", ob_v[k - 2], f"ot{k}")
                    if tt >= 2:
                        s.wait_tok("vector", ds_hist[tt - 2], f"ov{k}")
                    pe = s.emit("tensor", lambda e, tt=tt, ht=ht, pb=pb: e.transpose(
                        psTo[:, pb * 512: pb * 512 + 128],
                        stream[:, ht * T + tt * 128: ht * T + tt * 128 + 128],
                        id_f[:]), inc="PE", tag=f"ot{k}")
                    s.wait("vector", "PE", pe, f"ov{k}")
                    v = s.emit("vector", lambda e, sl=sl, ht=ht, pb=pb: e.tensor_copy(
                        obuf[:, sl * DM + ht * 128: sl * DM + ht * 128 + 128],
                        psTo[:, pb * 512: pb * 512 + 128]), inc="V", tag=f"ov{k}")
                    ob_v.append(v)
                s.wait("sync", "V", ob_v[-1], f"os{tt}")
                ds_hist.append(dma("sync", y_out[tt * 128:(tt + 1) * 128, :],
                                   obuf[:, sl * DM:(sl + 1) * DM], "DS", f"os{tt}"))
            for tk in ds_hist[-8:]:
                s.wait_tok("sync", tk, "final")

        s.check()
        s.dispatch()
    return nc



def prep_weight_maps(inp):
    m = {}
    m["wt"] = tile_ffn_weights(np.asarray(inp["L0_ffn_w1"], np.float32),
                               np.asarray(inp["L0_ffn_w2"], np.float32))
    g = np.asarray(inp["L0_ffn_ln_g"], np.float32)
    b = np.asarray(inp["L0_ffn_ln_b"], np.float32)
    m["gcol"] = np.ascontiguousarray(g.reshape(4, 4, 8, 128).transpose(3, 0, 1, 2).reshape(128, 128))
    m["bcol"] = np.ascontiguousarray(b.reshape(4, 4, 8, 128).transpose(3, 0, 1, 2).reshape(128, 128))
    b1 = np.asarray(inp["L0_ffn_b1"], np.float32)
    m["b1c"] = np.ascontiguousarray(b1.reshape(4, 4, 32, 128).transpose(3, 0, 1, 2).reshape(128, 512))
    b2 = np.asarray(inp["L0_ffn_b2"], np.float32)
    m["b2c"] = np.ascontiguousarray(b2.reshape(4, 4, 8, 128).transpose(3, 0, 1, 2).reshape(128, 128))
    rw = np.asarray(inp["L0_ffn_router_w"], np.float32)
    m["rwc"] = np.ascontiguousarray(rw.reshape(4, 8, 128, 4).transpose(2, 0, 1, 3).reshape(128, 128))
    m["rbc"] = np.ascontiguousarray(np.asarray(inp["L0_ffn_router_b"], np.float32).T)
    wdq = np.asarray(inp["L0_attn_W_DQ"], np.float32)
    m["wdq"] = np.ascontiguousarray(wdq.reshape(8, 128, 32, 128).transpose(2, 1, 0, 3).reshape(32, 128, 1024))
    wuq = np.asarray(inp["L0_attn_W_UQ"], np.float32)
    m["wuq"] = np.ascontiguousarray(wuq.reshape(4, 8, 128, 8, 128).transpose(3, 0, 2, 1, 4).reshape(32, 128, 1024))
    wqr = np.asarray(inp["L0_attn_W_QR"], np.float32)
    wqrp = np.zeros((4096, 576), np.float32)
    for h in range(NH):
        gg = h // 3
        off = (h % 3) * 32
        wqrp[:, gg * 96 + off: gg * 96 + off + 32] = wqr[:, h * 32:(h + 1) * 32]
    m["wqr"] = np.ascontiguousarray(wqrp.reshape(4, 8, 128, 6, 96).transpose(3, 0, 2, 1, 4).reshape(24, 128, 768))
    wdkv = np.asarray(inp["L0_attn_W_DKV"], np.float32)
    m["wdkv"] = np.ascontiguousarray(wdkv.reshape(8, 128, 2, 128).transpose(2, 1, 0, 3).reshape(2, 128, 1024))
    wuk = np.asarray(inp["L0_attn_W_UK"], np.float32)
    m["wuk"] = np.ascontiguousarray(wuk.reshape(2, 128, 8, 128).transpose(1, 0, 2, 3).reshape(128, 2048))
    wuv = np.asarray(inp["L0_attn_W_UV"], np.float32)
    m["wuv"] = np.ascontiguousarray(wuv.reshape(2, 128, 8, 128).transpose(1, 0, 2, 3).reshape(128, 2048))
    wkr = np.asarray(inp["L0_attn_W_KR"], np.float32)
    m["wkr"] = np.ascontiguousarray(wkr.reshape(8, 128, 32).transpose(1, 0, 2).reshape(128, 256))
    wo = np.asarray(inp["L0_attn_W_O"], np.float32)
    m["wo"] = np.ascontiguousarray(wo.reshape(8, 128, 8, 128).transpose(2, 1, 0, 3).reshape(8, 128, 1024))
    return m


def kernel(**inputs):
    x = np.asarray(inputs["L0_attn_x"], np.float32)
    wm = prep_weight_maps(inputs)
    nc = build_program()
    in_maps = []
    for i in range(8):
        d = dict(wm)
        d["x_in"] = np.ascontiguousarray(x[i])
        in_maps.append(d)
    res = run_bass_kernel_spmd(nc, in_maps, core_ids=list(range(8)))
    out = np.stack([np.asarray(r["y_out"], np.float32) for r in res.results], axis=0)
    return out


if __name__ == "__main__":
    import time
    t0 = time.time()
    nc = build_program()
    print(f"build ok: {time.time()-t0:.1f}s n_inst={nc.n_instructions()}")


# version 11 (final)
# speedup vs baseline: 1.1788x; 1.1788x over previous
import math
from contextlib import ExitStack

import numpy as np
import ml_dtypes
import concourse.bass as bass
import concourse.mybir as mybir
from concourse.bass_utils import run_bass_kernel_spmd

F32 = mybir.dt.float32
BF16 = mybir.dt.bfloat16
AF = mybir.ActivationFunctionType
ALU = mybir.AluOpType
AX = mybir.AxisListType.X

T = 2048
HB = 1024
NL = 4
NE = 4
DM = 1024
NH = 16
HS = 64
DR = 32
EPS = 1e-5
ISQ = 1.0 / math.sqrt(HS + DR)

bfnp = ml_dtypes.bfloat16
ENGS = ("tensor", "vector", "scalar", "gpsimd", "sync")
SEMS = ("PE", "V", "S", "P", "GEL", "EVC", "W1C", "W2C") + tuple(
    f"DW{j}" for j in range(8)) + tuple(f"DS{j}" for j in range(8))


class Sched:
    def __init__(self, nc):
        self.nc = nc
        self.thunks = {e: [] for e in ENGS}
        self.sems = {}
        self.counts = {}
        self.last_wait = {}
        self.ms = {}
        self.dma_idx = {"DW": 0, "DS": 0}
        self.last_inc = {}
        for sn in SEMS:
            self.sem(sn)

    def sem(self, name):
        if name not in self.sems:
            self.sems[name] = self.nc.semaphore(name).__enter__()
            self.counts[name] = 0
        return self.sems[name]

    def emit(self, eng, fn, inc=None, inc_n=1, tag="", inc2=None):
        if eng in ("vector", "scalar") and eng in self.last_inc:
            psem, pv = self.last_inc[eng]
            self.wait(eng, psem, pv, tag=f"chain_{tag}")
        if inc is None:
            self.thunks[eng].append({"fn": fn, "waits": [], "incs": [], "tag": tag})
            return None
        sem = self.sem(inc)
        self.counts[inc] += inc_n
        incs = [(inc, inc_n)]
        if inc2 is not None:
            sem2 = self.sem(inc2[0])
            self.counts[inc2[0]] += inc2[1]
            incs.append((inc2[0], inc2[1]))
            self.thunks[eng].append({
                "fn": (lambda e, fn=fn, sem=sem, n=inc_n, s2=sem2, n2=inc2[1]:
                       fn(e).then_inc(sem, n).then_inc(s2, n2)),
                "waits": [], "incs": incs, "tag": tag})
        else:
            self.thunks[eng].append({
                "fn": (lambda e, fn=fn, sem=sem, n=inc_n: fn(e).then_inc(sem, n)),
                "waits": [], "incs": incs, "tag": tag})
        if eng in ("vector", "scalar", "gpsimd"):
            self.last_inc[eng] = (inc, self.counts[inc])
        return self.counts[inc]

    def wait(self, eng, name, val, tag=""):
        if val is None or val <= 0:
            return
        key = (eng, name)
        if val <= self.last_wait.get(key, 0):
            return
        self.last_wait[key] = val
        sem = self.sem(name)
        self.thunks[eng].append({
            "fn": (lambda e, sem=sem, val=val: e.wait_ge(sem, val)),
            "waits": [(name, val)], "incs": [], "tag": tag})

    def mark(self, name, semname):
        self.ms[name] = (semname, self.counts[semname])

    def wait_tok(self, eng, tok, tag=""):
        if tok is None:
            return
        self.wait(eng, tok[0], tok[1], tag=tag)

    def wait_ms(self, eng, name, tag=""):
        semname, v = self.ms[name]
        self.wait(eng, semname, v, tag=tag or name)

    def fence(self, name):
        for sn in SEMS:
            self.ms[f"{name}:{sn}"] = (sn, self.counts[sn])

    def wait_fence(self, eng, name, tag=""):
        for sn in SEMS:
            self.wait_ms(eng, f"{name}:{sn}", tag=tag or name)

    def check(self):
        ptr = {e: 0 for e in ENGS}
        vals = {k: 0 for k in self.counts}
        progress = True
        while progress:
            progress = False
            for e in ENGS:
                while ptr[e] < len(self.thunks[e]):
                    th = self.thunks[e][ptr[e]]
                    if all(vals.get(sn, 0) >= v for sn, v in th["waits"]):
                        for sn, n in th["incs"]:
                            vals[sn] = vals.get(sn, 0) + n
                        ptr[e] += 1
                        progress = True
                    else:
                        break
        stuck = {e: ptr[e] for e in ENGS if ptr[e] < len(self.thunks[e])}
        if stuck:
            lines = []
            for e, p in stuck.items():
                th = self.thunks[e][p]
                lines.append(f"  {e}[{p}/{len(self.thunks[e])}] tag={th['tag']} waits={th['waits']}")
            raise RuntimeError("DEADLOCK in schedule:\n" + "\n".join(lines))

    def dispatch(self):
        with self.nc.Block() as block:
            @block.tensor
            def _(eng):
                for th in self.thunks["tensor"]:
                    th["fn"](eng)

            @block.vector
            def _(eng):
                for th in self.thunks["vector"]:
                    th["fn"](eng)

            @block.scalar
            def _(eng):
                for th in self.thunks["scalar"]:
                    th["fn"](eng)

            @block.gpsimd
            def _(eng):
                for th in self.thunks["gpsimd"]:
                    th["fn"](eng)

            @block.sync
            def _(eng):
                for th in self.thunks["sync"]:
                    th["fn"](eng)


def tile_ffn_weights(w1, w2):
    L, E = w1.shape[0], w1.shape[1]
    out = np.empty((L, E, 32, 128, 2048), np.float32)
    t1 = w1.reshape(L, E, 8, 128, 16, 256).transpose(0, 1, 4, 3, 2, 5)
    out[:, :, 0:16] = t1.reshape(L, E, 16, 128, 2048)
    t2 = w2.reshape(L, E, 2, 16, 128, 8, 128).transpose(0, 1, 5, 2, 4, 3, 6)
    out[:, :, 16:32] = t2.reshape(L, E, 16, 128, 2048)
    return np.ascontiguousarray(out)


def build_consts():
    c = {}
    c["id_f32"] = np.eye(128, dtype=np.float32)
    c["id_bf16"] = np.eye(128, dtype=bfnp)
    tri = np.where(np.arange(128)[:, None] >= np.arange(128)[None, :], 0.0, -1e30)
    c["tri"] = tri.astype(bfnp)
    ones = np.zeros((65, 128), dtype=bfnp)
    ones[0, :] = 1.0
    ones[32, :] = 1.0
    ones[64, :] = 1.0
    c["onesb"] = ones
    sc = np.zeros((128, 2), dtype=np.float32)
    sc[:, 0] = -1.0 / 1024.0
    sc[:, 1] = 1.0 / 1024.0
    c["statlhs"] = sc.astype(bfnp)
    oh = np.zeros((4, 4 * 128), dtype=bfnp)
    for e in range(4):
        oh[e, e * 128:(e + 1) * 128] = 1.0
    c["oh4"] = oh
    t = np.arange(T, dtype=np.float32)
    inv = (1.0 / 10000.0 ** (np.arange(0, DR, 2, dtype=np.float32) / DR)).astype(np.float32)
    freqs = t[:, None] * inv[None, :]
    emb = np.concatenate([freqs, freqs], -1).astype(np.float32)
    scale = (np.cos(emb) + np.sin(emb)).astype(np.float32)
    qrope = np.empty((96, T), np.float32)
    for p in range(96):
        qrope[p] = scale[:, p % 32]
    c["qrope"] = qrope.astype(bfnp)
    return c


def build_program(depth=NL, do_attn=True, do_ffn2=True, stop=None, dbg=False, dbg_call=0, chain_trunc=None):
    nc = bass.Bass("TRN2", target_bir_lowering=False)
    consts = build_consts()

    x_in = nc.dram_tensor("x_in", [T, DM], F32, kind="ExternalInput").ap()
    wt = nc.dram_tensor("wt", [NL, NE, 32, 128, 2048], F32, kind="ExternalInput").ap()
    gcol_d = nc.dram_tensor("gcol", [128, 128], F32, kind="ExternalInput").ap()
    bcol_d = nc.dram_tensor("bcol", [128, 128], F32, kind="ExternalInput").ap()
    b1c_d = nc.dram_tensor("b1c", [128, 512], F32, kind="ExternalInput").ap()
    b2c_d = nc.dram_tensor("b2c", [128, 128], F32, kind="ExternalInput").ap()
    rwc_d = nc.dram_tensor("rwc", [128, 128], F32, kind="ExternalInput").ap()
    rbc_d = nc.dram_tensor("rbc", [4, 4], F32, kind="ExternalInput").ap()
    wdq_d = nc.dram_tensor("wdq", [32, 128, 1024], F32, kind="ExternalInput").ap()
    wuq_d = nc.dram_tensor("wuq", [32, 128, 1024], F32, kind="ExternalInput").ap()
    wqr_d = nc.dram_tensor("wqr", [24, 128, 768], F32, kind="ExternalInput").ap()
    wdkv_d = nc.dram_tensor("wdkv", [2, 128, 1024], F32, kind="ExternalInput").ap()
    wuk_d = nc.dram_tensor("wuk", [128, 2048], F32, kind="ExternalInput").ap()
    wuv_d = nc.dram_tensor("wuv", [128, 2048], F32, kind="ExternalInput").ap()
    wkr_d = nc.dram_tensor("wkr", [128, 256], F32, kind="ExternalInput").ap()
    wo_d = nc.dram_tensor("wo", [8, 128, 1024], F32, kind="ExternalInput").ap()
    y_out = nc.dram_tensor("y_out", [T, DM], F32, kind="ExternalOutput").ap()
    if dbg:
        dbg_comb = nc.dram_tensor("dbg_comb", [4, T], F32, kind="ExternalOutput").ap()
        dbg_zhat = nc.dram_tensor("dbg_zhat", [128, 8 * HB], F32, kind="ExternalOutput").ap()
        dbg_xe = nc.dram_tensor("dbg_xe", [128, 8 * HB], F32, kind="ExternalOutput").ap()
        dbg_z = nc.dram_tensor("dbg_z", [128, 8 * T], F32, kind="ExternalOutput").ap()
    xspill = nc.dram_tensor("xspill", [128, 8 * T], F32, kind="Internal").ap()
    wt1_bf = nc.dram_tensor("wt1_bf", [NL, NE, 16, 128, 2048], BF16, kind="Internal").ap()
    wt2_bf = nc.dram_tensor("wt2_bf", [NL, NE, 16, 128, 2048], BF16, kind="Internal").ap()
    awq_bf = {}
    for nm, shp in [("wdq", [32, 128, 1024]), ("wuq", [32, 128, 1024]),
                    ("wqr", [24, 128, 768]), ("wdkv", [2, 128, 1024]),
                    ("wuk", [128, 2048]), ("wuv", [128, 2048]),
                    ("wkr", [128, 256]), ("wo", [8, 128, 1024])]:
        awq_bf[nm] = nc.dram_tensor(nm + "_bf", shp, BF16, kind="Internal").ap()

    cst = {k: nc.inline_tensor(v, f"c_{k}") for k, v in consts.items()}
    fconst_np = np.array([[EPS, -0.5, ISQ, 0.0]], np.float32)
    fconst_d = nc.inline_tensor(fconst_np, "c_fconst")

    with ExitStack() as top:
        Et = top.enter_context
        stream = Et(nc.sbuf_tensor("stream", [128, 8 * T], F32))
        rows = Et(nc.sbuf_tensor("rows", [66, T], BF16))
        id_f = Et(nc.sbuf_tensor("id_f", [128, 128], F32))
        id_b = Et(nc.sbuf_tensor("id_b", [128, 128], BF16))
        tri = Et(nc.sbuf_tensor("tri", [128, 128], BF16))
        onesb = Et(nc.sbuf_tensor("onesb", [65, 128], BF16))
        statlhs = Et(nc.sbuf_tensor("statlhs", [128, 2], BF16))
        oh4 = Et(nc.sbuf_tensor("oh4", [4, 512], BF16))
        smalls = Et(nc.sbuf_tensor("smalls", [128, 16], F32))
        fconst = Et(nc.sbuf_tensor("fconst", [1, 4], F32))
        nc.const_aps.register(F32, EPS, fconst[0:1, 0:1])
        nc.const_aps.register(F32, -0.5, fconst[0:1, 1:2])
        nc.const_aps.register(F32, ISQ, fconst[0:1, 2:3])

        s = Sched(nc)

        def dma(eng, out_ap, in_ap, group, tag=""):
            idx = s.dma_idx[group]
            s.dma_idx[group] = idx + 1
            lane = f"{group}{idx % 8}"
            s.wait(eng, lane, 16 * (idx // 8), tag=f"{tag}_pred")
            cnt = s.emit(eng, lambda e, o=out_ap, i=in_ap: e.dma_start(o, i),
                         inc=lane, inc_n=16, tag=tag)
            return (lane, cnt)

        const_toks = [
            dma("sync", id_f[:], cst["id_f32"].ap(), "DS", "c"),
            dma("sync", id_b[:], cst["id_bf16"].ap(), "DS", "c"),
            dma("sync", tri[:], cst["tri"].ap(), "DS", "c"),
            dma("sync", onesb[:], cst["onesb"].ap(), "DS", "c"),
            dma("sync", statlhs[:], cst["statlhs"].ap(), "DS", "c"),
            dma("sync", oh4[:], cst["oh4"].ap(), "DS", "c"),
            dma("sync", fconst[:], fconst_d.ap(), "DS", "c"),
        ]
        for tk in const_toks:
            s.wait_tok("tensor", tk, "consts")
            s.wait_tok("vector", tk, "consts")
            s.wait_tok("scalar", tk, "consts")

        with ExitStack() as sc0:
            xin = sc0.enter_context(nc.sbuf_tensor("xin", [128, 2 * DM], F32))
            psT0 = sc0.enter_context(nc.psum_tensor("psT0", [128, 1024], F32))
            vhist = []
            pehist = []
            for tt in range(16):
                sl = tt % 2
                if tt >= 2:
                    s.wait("sync", "PE", pehist[tt - 2], f"ix{tt}")
                dtok = dma("sync", xin[:, sl * DM:(sl + 1) * DM],
                           x_in[tt * 128:(tt + 1) * 128, :], "DS", f"ix{tt}")
                s.wait_tok("tensor", dtok, f"ix{tt}")
                for ht in range(8):
                    k = tt * 8 + ht
                    pb = k % 2
                    if k >= 2:
                        s.wait("tensor", "V", vhist[k - 2], f"ixp{k}")
                    pe = s.emit("tensor", lambda e, sl=sl, ht=ht, pb=pb: e.transpose(
                        psT0[:, pb * 512:pb * 512 + 128],
                        xin[:, sl * DM + ht * 128: sl * DM + ht * 128 + 128],
                        id_f[:]), inc="PE", tag=f"ixt{k}")
                    s.wait("vector", "PE", pe, f"ixv{k}")
                    v = s.emit("vector", lambda e, tt=tt, ht=ht, pb=pb: e.tensor_copy(
                        stream[:, ht * T + tt * 128: ht * T + tt * 128 + 128],
                        psT0[:, pb * 512:pb * 512 + 128]), inc="V", tag=f"ixv{k}")
                    vhist.append(v)
                pehist.append(s.counts["PE"])
            s.mark("x_loaded", "V")
        s.fence("postload")

        def emit_ln_chain(tag, half, B, dep_ms_list, router=None, dst_fn=None,
                          psc_v=None, psdt_free_ms=None, defer_norm=False, trunc=None):
            zbf, scr_t, psC = B["zbf"], B["scr_t"], B["psC"]
            psDt = B.get("psDt")
            t0h = half * HB
            router_orig = router
            do_comb = router is not None and trunc != "stats"
            lt_pe_hist = [None, None]
            if psc_v is not None:
                s.wait("tensor", "V", psc_v[0], f"{tag}_pscf")
            z2war = B.get("z2war")
            if z2war is not None:
                s.wait("scalar", "V", z2war[0], f"{tag}_z2war")
            for sub in range(8):
                t0 = t0h + sub * 128
                bank = sub % 2
                if sub >= 1:
                    s.wait_ms("scalar", f"{tag}_st{sub-1}", f"{tag}_zwar{sub}")
                for dep in dep_ms_list:
                    s.wait_ms("scalar", dep, f"{tag}_dep")
                for ht in range(8):
                    s.emit("scalar", lambda e, ht=ht, t0=t0: e.copy(
                        zbf[:, ht * 128: ht * 128 + 128],
                        stream[:, ht * T + t0: ht * T + t0 + 128]), inc="S", tag=f"{tag}_zc{sub}")
                s.mark(f"{tag}_zc{sub}", "S")
                s.wait_ms("tensor", f"{tag}_zc{sub}")
                if sub >= 2:
                    s.wait_ms("tensor", f"{tag}_rc{sub-2}", f"{tag}_bw{sub}")
                    if router is not None:
                        s.wait_ms("tensor", f"{tag}_lgv{sub-2}", f"{tag}_bw{sub}")
                for ht in range(8):
                    s.emit("tensor", lambda e, ht=ht, bank=bank: e.matmul(
                        psC[0:1, bank * 512: bank * 512 + 128],
                        statlhs[:, 0:1], zbf[:, ht * 128: ht * 128 + 128],
                        start=(ht == 0), stop=(ht == 7)),
                        inc=("PE" if ht == 7 else None), tag=f"{tag}_mu{sub}")
                if router is not None:
                    for ht in range(8):
                        s.emit("tensor", lambda e, ht=ht, bank=bank, l=router: e.matmul(
                            psC[0:4, bank * 512 + 256: bank * 512 + 384],
                            B["rwc"][:, (l * 8 + ht) * 4:(l * 8 + ht) * 4 + 4],
                            zbf[:, ht * 128: ht * 128 + 128],
                            start=(ht == 0), stop=(ht == 7)),
                            inc=("PE" if ht == 7 else None), tag=f"{tag}_rt{sub}")
                s.mark(f"{tag}_st1_{sub}", "PE")
                s.wait_ms("scalar", f"{tag}_st1_{sub}")
                for ht in range(8):
                    s.emit("scalar", lambda e, ht=ht, t0=t0: e.activation(
                        zbf[:, ht * 128: ht * 128 + 128],
                        stream[:, ht * T + t0: ht * T + t0 + 128], AF.Square),
                        inc="S", tag=f"{tag}_z2{sub}")
                s.mark(f"{tag}_zq{sub}", "S")
                s.wait_ms("tensor", f"{tag}_zq{sub}")
                for ht in range(8):
                    s.emit("tensor", lambda e, ht=ht, bank=bank: e.matmul(
                        psC[0:1, bank * 512 + 128: bank * 512 + 256],
                        statlhs[:, 1:2], zbf[:, ht * 128: ht * 128 + 128],
                        start=(ht == 0), stop=(ht == 7)),
                        inc=("PE" if ht == 7 else None), tag=f"{tag}_e2{sub}")
                s.mark(f"{tag}_st{sub}", "PE")
                s.wait_ms("scalar", f"{tag}_st{sub}")
                s.emit("scalar", lambda e, t0=t0, bank=bank: e.copy(
                    rows[0:1, t0: t0 + 128], psC[0:1, bank * 512: bank * 512 + 128]),
                    inc="S", tag=f"{tag}_rmu{sub}")
                s.emit("scalar", lambda e, t0=t0, bank=bank: e.copy(
                    rows[64:65, t0: t0 + 128], psC[0:1, bank * 512 + 128: bank * 512 + 256]),
                    inc="S", tag=f"{tag}_re2{sub}")
                s.mark(f"{tag}_rc{sub}", "S")
                if router is not None:
                    s.wait_ms("vector", f"{tag}_rc{sub}")
                    if lt_pe_hist[sub % 2] is not None:
                        s.wait("vector", "PE", lt_pe_hist[sub % 2], f"{tag}_lgw{sub}")
                    s.emit("vector", lambda e, sub=sub, bank=bank, l=router: e.tensor_scalar_add(
                        B["lgsb"][0:4, (sub % 2) * 128: (sub % 2) * 128 + 128],
                        psC[0:4, bank * 512 + 256: bank * 512 + 384],
                        B["rbc"][0:4, l:l + 1]), inc="V", tag=f"{tag}_lg{sub}")
                    s.mark(f"{tag}_lgv{sub}", "V")
                    if do_comb:
                        if sub == 0 and psdt_free_ms:
                            s.wait_ms("tensor", psdt_free_ms, f"{tag}_psdtf")
                        s.wait_ms("tensor", f"{tag}_lgv{sub}")
                        lt_pe_hist[sub % 2] = s.emit("tensor", lambda e, sub=sub: e.transpose(
                            psDt[:, sub * 4: sub * 4 + 4],
                            B["lgsb"][0:4, (sub % 2) * 128: (sub % 2) * 128 + 128],
                            id_b[0:4, 0:4]), inc="PE", tag=f"{tag}_lt{sub}")
                        if sub == 7:
                            s.mark(f"{tag}_lt", "PE")
            s.mark(f"{tag}_rcdone", "S")
            s.wait_ms("vector", f"{tag}_rcdone")
            s.emit("vector", lambda e: e.tensor_tensor(
                zbf[64:65, 0:HB], rows[0:1, t0h:t0h + HB],
                rows[0:1, t0h:t0h + HB], ALU.mult), inc="V", tag=f"{tag}_mu2")
            vv = s.emit("vector", lambda e: e.tensor_tensor(
                rows[32:33, t0h:t0h + HB], rows[64:65, t0h:t0h + HB],
                zbf[64:65, 0:HB], ALU.subtract), inc="V", tag=f"{tag}_var")
            if z2war is not None:
                z2war[0] = vv
            s.mark(f"{tag}_var", "V")
            s.wait_ms("scalar", f"{tag}_var")
            s.emit("scalar", lambda e: e.activation(
                rows[32:33, t0h:t0h + HB], rows[32:33, t0h:t0h + HB], AF.Ln, bias=EPS),
                inc="S", tag=f"{tag}_lnr")
            s.emit("scalar", lambda e: e.activation(
                rows[32:33, t0h:t0h + HB], rows[32:33, t0h:t0h + HB], AF.Exp, scale=-0.5),
                inc="S", tag=f"{tag}_expr")
            s.mark(f"{tag}_rows", "S")

            if router is not None and trunc == "stats":
                router = None
            if router is not None:
                lgsb, ltok, cwork, comb_rows = B["lgsb"], B["ltok"], B["cwork"], B["comb_rows"]
                s.wait_ms("vector", f"{tag}_lt")
                s.emit("vector", lambda e: e.tensor_copy(
                    ltok[:].rearrange("p (g t) -> p t g", g=4),
                    psDt[:, 0:32].rearrange("p (t g) -> p t g", g=4)), inc="V", tag=f"{tag}_ltc")
                s.emit("vector", lambda e: e.tensor_reduce(
                    cwork[:, 64:65], ltok[:], AX, ALU.max, negate=True), inc="V", tag=f"{tag}_max")
                s.mark(f"{tag}_max", "V")
                s.wait_ms("scalar", f"{tag}_max")
                s.emit("scalar", lambda e: e.activation(
                    cwork[:, 0:32], ltok[:], AF.Exp, bias=cwork[:, 64:65]),
                    inc="S", tag=f"{tag}_cexp")
                s.mark(f"{tag}_cexp", "S")
                s.wait_ms("vector", f"{tag}_cexp")
                if trunc == "exp":
                    s.emit("vector", lambda e: e.tensor_copy(
                        ltok[:], cwork[:, 0:32]), inc="V", tag=f"{tag}_tr_exp")

                def le(a):
                    return ltok[:, a * 8:(a + 1) * 8]

                def cw(a):
                    return cwork[:, 32 + a * 8: 32 + (a + 1) * 8]

                for a in (range(0) if trunc == "exp" else range(4)):
                    first = True
                    for b in range(4):
                        if a == b:
                            continue
                        s.emit("vector", lambda e, a=a, b=b: e.tensor_tensor(
                            cwork[:, 72:80], le(b), le(a), ALU.is_gt), inc="V", tag=f"{tag}_gt")
                        if first:
                            s.emit("vector", lambda e, a=a: e.tensor_copy(
                                cw(a), cwork[:, 72:80]), inc="V", tag=f"{tag}_cc")
                            first = False
                        else:
                            s.emit("vector", lambda e, a=a: e.tensor_tensor(
                                cw(a), cw(a), cwork[:, 72:80], ALU.add), inc="V", tag=f"{tag}_ca")
                for a in (range(0) if trunc == "exp" else range(4)):
                    s.emit("vector", lambda e, a=a: e.tensor_single_scalar(
                        cw(a), cw(a), 1.5, ALU.is_lt), inc="V", tag=f"{tag}_mk")
                    s.emit("vector", lambda e, a=a: e.tensor_tensor(
                        cw(a), cw(a), cwork[:, a * 8:(a + 1) * 8], ALU.mult), inc="V", tag=f"{tag}_wm")
                if trunc == "exp":
                    pass
                else:
                    s.emit("vector", lambda e: e.tensor_tensor(
                        cwork[:, 72:80], cw(0), cw(1), ALU.add), inc="V", tag=f"{tag}_d0")
                if trunc != "exp":
                    s.emit("vector", lambda e: e.tensor_tensor(
                        cwork[:, 72:80], cwork[:, 72:80], cw(2), ALU.add), inc="V", tag=f"{tag}_d1")
                    s.emit("vector", lambda e: e.tensor_tensor(
                        cwork[:, 72:80], cwork[:, 72:80], cw(3), ALU.add), inc="V", tag=f"{tag}_d2")
                    s.emit("vector", lambda e: e.reciprocal(
                        cwork[:, 80:88], cwork[:, 72:80]), inc="V", tag=f"{tag}_rcp")
                    for a in range(4):
                        s.emit("vector", lambda e, a=a: e.tensor_tensor(
                            le(a), cw(a), cwork[:, 80:88], ALU.mult), inc="V", tag=f"{tag}_cb")
                s.mark(f"{tag}_comb", "V")
                s.wait_ms("tensor", f"{tag}_comb")
                for j in range(8):
                    s.emit("tensor", lambda e, j=j: e.transpose(
                        psDt[0:4, j * 128: j * 128 + 128],
                        ltok[:].rearrange("p (g t) -> p t g", g=4)[:, j],
                        id_b[:]), inc=("PE" if j == 7 else None), tag=f"{tag}_ct{j}")
                s.mark(f"{tag}_ct", "PE")
                s.wait_ms("vector", f"{tag}_ct")
                s.emit("vector", lambda e, t0h=t0h: e.tensor_copy(
                    comb_rows[0:4, t0h:t0h + HB], psDt[0:4, 0:1024]), inc="V", tag=f"{tag}_crc")
                s.mark(f"{tag}_combr", "V")

            def emit_norm():
                if psc_v is not None:
                    s.wait("tensor", "V", psc_v[0], f"{tag}_pscn")
                for ch in range(2):
                    tc = t0h + ch * 512
                    s.wait_ms("tensor", f"{tag}_rows")
                    if ch >= 1:
                        s.wait_ms("tensor", f"{tag}_nrm0", f"{tag}_nw")
                    else:
                        s.wait_ms("tensor", f"{tag}_rc6", f"{tag}_nw")
                        s.wait_ms("tensor", f"{tag}_rc7", f"{tag}_nw")
                        if router_orig is not None:
                            s.wait_ms("tensor", f"{tag}_lgv6", f"{tag}_nw")
                            s.wait_ms("tensor", f"{tag}_lgv7", f"{tag}_nw")
                    s.emit("tensor", lambda e, tc=tc: e.matmul(
                        psC[:, 0:512], onesb[0:1, :], rows[0:1, tc:tc + 512],
                        start=True, stop=True), tag=f"{tag}_bmu{ch}")
                    s.emit("tensor", lambda e, tc=tc: e.matmul(
                        psC[:, 512:1024], onesb[32:33, :], rows[32:33, tc:tc + 512],
                        start=True, stop=True), inc="PE", tag=f"{tag}_brs{ch}")
                    s.mark(f"{tag}_bc{ch}", "PE")
                    s.wait_ms("vector", f"{tag}_bc{ch}")
                    for ht in range(8):
                        s.emit("vector", lambda e, ht=ht, tc=tc: e.tensor_tensor(
                            scr_t[:, 0:512], stream[:, ht * T + tc: ht * T + tc + 512],
                            psC[:, 0:512], ALU.add), inc="V", tag=f"{tag}_n1_{ch}_{ht}")
                        s.emit("vector", lambda e, o=dst_fn(ht, ch): e.tensor_tensor(
                            o, scr_t[:, 0:512], psC[:, 512:1024], ALU.mult),
                            inc="V", tag=f"{tag}_n2_{ch}_{ht}")
                    s.mark(f"{tag}_nrm{ch}", "V")
                s.mark(f"{tag}_done", "V")
                if psc_v is not None:
                    psc_v[0] = s.counts["V"]
            if defer_norm:
                return emit_norm
            emit_norm()

        def emit_ffn(call, pre_fence):
            with ExitStack() as sc:
                Fa = sc.enter_context
                zhat = Fa(nc.sbuf_tensor(f"zhat{call}", [128, 8 * HB], BF16))
                xe = Fa(nc.sbuf_tensor(f"xe{call}", [128, 8 * HB], BF16))
                hact = Fa(nc.sbuf_tensor(f"hact{call}", [128, 32 * HB], BF16))
                w1b = Fa(nc.sbuf_tensor(f"w1b{call}", [128, 3 * 2048], BF16))
                w2b = Fa(nc.sbuf_tensor(f"w2b{call}", [128, 3 * 2048], BF16))
                zbf = Fa(nc.sbuf_tensor(f"zbf{call}", [128, 8 * 128], BF16))
                scr_t = Fa(nc.sbuf_tensor(f"scrt{call}", [128, 512], F32))
                bcast = Fa(nc.sbuf_tensor(f"bcast{call}", [128, 2 * HB], BF16))
                comb_rows = Fa(nc.sbuf_tensor(f"combr{call}", [4, T], BF16))
                lgsb = Fa(nc.sbuf_tensor(f"lgsb{call}", [4, 256], BF16))
                ltok = Fa(nc.sbuf_tensor(f"ltok{call}", [128, 32], BF16))
                cwork = Fa(nc.sbuf_tensor(f"cwork{call}", [128, 96], F32))
                gcol = Fa(nc.sbuf_tensor(f"gcol{call}", [128, 128], F32))
                bcol = Fa(nc.sbuf_tensor(f"bcol{call}", [128, 128], F32))
                b1c = Fa(nc.sbuf_tensor(f"b1c{call}", [128, 512], F32))
                b2c = Fa(nc.sbuf_tensor(f"b2c{call}", [128, 128], F32))
                rwc = Fa(nc.sbuf_tensor(f"rwcs{call}", [128, 128], BF16))
                rbc = Fa(nc.sbuf_tensor(f"rbcs{call}", [4, 4], F32))
                psU = Fa(nc.psum_tensor(f"psU{call}", [128, 2560], F32))
                psC = Fa(nc.psum_tensor(f"psC{call}", [128, 1024], F32))
                psDt = Fa(nc.psum_tensor(f"psDt{call}", [128, 1024], BF16))

                B = {"zbf": zbf, "scr_t": scr_t, "psC": psC, "psDt": psDt,
                     "lgsb": lgsb, "ltok": ltok, "cwork": cwork, "comb_rows": comb_rows,
                     "rbc": rbc, "rwc": rwc, "z2war": [0]}

                s.wait_fence("gpsimd", pre_fence, f"f{call}")
                tok_gcol = dma("gpsimd", gcol[:], gcol_d, "DW", "aux")
                tok_bcol = dma("gpsimd", bcol[:], bcol_d, "DW", "aux")
                tok_b1c = dma("gpsimd", b1c[:], b1c_d, "DW", "aux")
                tok_b2c = dma("gpsimd", b2c[:], b2c_d, "DW", "aux")
                tok_rwc = dma("gpsimd", rwc[:], rwc_d, "DW", "aux")
                s.wait_fence("sync", pre_fence, f"f{call}")
                tok_rbc = dma("sync", rbc[:], rbc_d, "DS", "aux")
                tok_spill = dma("sync", xspill, stream[:], "DS", f"spill{call}")

                s.wait_tok("vector", tok_spill, "lnsp")
                s.wait_tok("vector", tok_rbc, "lnaux")
                s.wait_tok("vector", tok_gcol, "lnaux")
                s.wait_tok("vector", tok_bcol, "lnaux")
                s.wait_tok("vector", tok_b2c, "lnaux")
                s.wait_tok("scalar", tok_b1c, "lnaux")
                s.wait_tok("tensor", tok_rwc, "lnaux")

                nslots = depth * 2
                nunits = nslots * NE
                if stop == "ln":
                    nslots = 0
                    nunits = 0
                psc_v = [0]

                for half in range(2):
                    emit_ln_chain(
                        f"M{call}_{half}", half, B, [],
                        router=None,
                        dst_fn=(lambda ht, ch, half=half: stream[
                            :, ht * T + half * HB + ch * 512: ht * T + half * HB + ch * 512 + 512]),
                        psc_v=psc_v)

                chain_state = {"psdt": None}
                if stop == "ln":
                    s.mark(f"merge{call}", "V")
                    s.fence(f"ffn{call}_end")
                    return

                pending_norm = {}

                def emit_chain(sl):
                    l, half = divmod(sl, 2)
                    tagc = f"C{call}_{l}_{half}"
                    deps = [f"zupd{call}_{l-1}_{half}"] if l > 0 else [f"M{call}_{half}_done"]
                    pending_norm[sl] = emit_ln_chain(
                        tagc, half, B, deps, router=l,
                        dst_fn=(lambda ht, ch: zhat[
                            :, ht * HB + ch * 512: ht * HB + ch * 512 + 512]),
                        psc_v=psc_v,
                        psdt_free_ms=chain_state["psdt"], defer_norm=True,
                        trunc=chain_trunc)
                    if chain_trunc != "stats":
                        chain_state["psdt"] = f"{tagc}_combr"

                def emit_xe(sl, e, s1_war):
                    l, half = divmod(sl, 2)
                    s.wait_ms("vector", f"C{call}_{l}_{half}_done", f"xe{sl}_{e}")
                    if s1_war is not None:
                        s.wait("vector", s1_war[0], s1_war[1], f"xew{sl}_{e}")
                    col = (l * 4 + e) * 8
                    for ht in range(8):
                        s.emit("vector", lambda e_, ht=ht, col=col: e_.tensor_scalar(
                            xe[:, ht * HB: (ht + 1) * HB], zhat[:, ht * HB:(ht + 1) * HB],
                            gcol[:, col + ht: col + ht + 1], bcol[:, col + ht: col + ht + 1],
                            ALU.mult, ALU.add), inc="V", tag=f"xe{sl}_{e}_{ht}")
                    s.mark(f"xe{call}_{sl}_{e}", "V")

                bct_prev_v = [0]

                def emit_bcast(sl, e):
                    l, half = divmod(sl, 2)
                    par = e % 2
                    s.wait_ms("tensor", f"C{call}_{l}_{half}_combr", f"bc{sl}_{e}")
                    s.wait("tensor", "V", psc_v[0], f"bcw{sl}_{e}")
                    for ch in range(2):
                        s.emit("tensor", lambda e_, e=e, half=half, ch=ch: e_.matmul(
                            psC[:, ch * 512: ch * 512 + 512],
                            oh4[:, e * 128:(e + 1) * 128],
                            comb_rows[0:4, half * HB + ch * 512: half * HB + ch * 512 + 512],
                            start=True, stop=True),
                            inc=("PE" if ch == 1 else None), tag=f"bct{sl}_{e}")
                    s.mark(f"bct{call}_{sl}_{e}", "PE")
                    s.wait_ms("vector", f"bct{call}_{sl}_{e}")
                    psc_v[0] = s.emit("vector", lambda e_, par=par: e_.tensor_copy(
                        bcast[:, par * HB:(par + 1) * HB], psC[:, 0:1024]),
                        inc="V", tag=f"bcc{sl}_{e}")
                    s.mark(f"bcs{call}_{sl}_{e}", "V")

                emit_chain(0)
                pending_norm.pop(0)()
                emit_chain(1)
                if chain_trunc is None:
                    emit_xe(0, 0, None)
                    emit_bcast(0, 0)
                if stop == "chain":
                    s.mark(f"merge{call}", "V")
                    s.fence(f"ffn{call}_end")
                    return

                w1_dw, w2_dw = [], []
                evt_hist = []
                conv_list = []
                if call == 0:
                    for l2 in range(depth):
                        for e2 in range(NE):
                            conv_list.append((wt1_bf[l2, e2], wt[l2, e2, 0:16], (l2, e2, 1)))
                            conv_list.append((wt2_bf[l2, e2], wt[l2, e2, 16:32], (l2, e2, 2)))
                    asrc = {"wdq": wdq_d, "wuq": wuq_d, "wqr": wqr_d, "wdkv": wdkv_d,
                            "wuk": wuk_d, "wuv": wuv_d, "wkr": wkr_d, "wo": wo_d}
                    for nm in asrc:
                        conv_list.append((awq_bf[nm], asrc[nm], ("a", nm)))
                gel_base = s.counts["GEL"]
                evc_base = s.counts["EVC"]
                w1c_base = s.counts["W1C"]
                w2c_base = s.counts["W2C"]

                conv_toks = getattr(build_program, "_conv_toks", {})
                build_program._conv_toks = conv_toks
                for sl in range(nslots):
                    l, half = divmod(sl, 2)
                    for e in range(NE):
                        u = sl * NE + e
                        if call == 0:
                            for _ in range(6 if u < 2 else (2 if u < 20 else 8)):
                                if conv_list:
                                    dst, src_, key = conv_list.pop(0)
                                    conv_toks[key] = dma("gpsimd", dst, src_, "DW", f"cv{key}")
                        use_scr = (call == 1) or ((l, e, 1) in conv_toks and (l * 4 + e) >= 3)
                        for sb in range(16):
                            g = u * 16 + sb
                            if use_scr:
                                if g >= 3:
                                    s.wait("sync", "W1C", w1c_base + g - 2, f"w1f{g}")
                                if sb == 0:
                                    s.wait_tok("sync", conv_toks[(l, e, 1)], f"w1cv{u}")
                                w1_dw.append(dma(
                                    "sync", w1b[:, (g % 3) * 2048:(g % 3) * 2048 + 2048],
                                    wt1_bf[l, e, sb], "DS", f"w1d{g}"))
                            else:
                                if g >= 3:
                                    s.wait("gpsimd", "W1C", w1c_base + g - 2, f"w1f{g}")
                                w1_dw.append(dma(
                                    "gpsimd", w1b[:, (g % 3) * 2048:(g % 3) * 2048 + 2048],
                                    wt[l, e, sb], "DW", f"w1d{g}"))
                        for sb in range(16):
                            g = u * 16 + sb
                            if g >= 3:
                                s.wait("gpsimd", "W2C", w2c_base + g - 2, f"w2f{g}")
                            if use_scr:
                                if sb == 0:
                                    s.wait_tok("gpsimd", conv_toks[(l, e, 2)], f"w2cv{u}")
                                w2_dw.append(dma(
                                    "gpsimd", w2b[:, (g % 3) * 2048:(g % 3) * 2048 + 2048],
                                    wt2_bf[l, e, sb], "DW", f"w2d{g}"))
                            else:
                                w2_dw.append(dma(
                                    "gpsimd", w2b[:, (g % 3) * 2048:(g % 3) * 2048 + 2048],
                                    wt[l, e, 16 + sb], "DW", f"w2d{g}"))

                        s1_pair = []
                        pslots = []
                        s.wait_ms("tensor", f"xe{call}_{sl}_{e}", f"s1x{u}")
                        for ft in range(32):
                            if ft % 2 == 0:
                                s.wait_tok("tensor", w1_dw[u * 16 + ft // 2], f"s1w{u}_{ft}")
                            for c in range(2):
                                P = u * 64 + 2 * ft + c
                                evt = len(evt_hist)
                                slot = evt % 5
                                if evt >= 5:
                                    s.wait("tensor", evt_hist[evt - 5][0], evt_hist[evt - 5][1],
                                           f"s1g{u}_{ft}_{c}")
                                evt_hist.append(("GEL", gel_base + P + 1))
                                pslots.append(slot)
                                for ht in range(8):
                                    semn = None
                                    if ht == 7:
                                        semn = "W1C" if (c == 1 and ft % 2 == 1) else "PE"
                                    v = s.emit("tensor", lambda e_, g=(u * 16 + ft // 2), ht=ht, ft=ft, c=c, slot=slot: e_.matmul(
                                        psU[:, slot * 512: slot * 512 + 512],
                                        w1b[:, (g % 3) * 2048 + ht * 256 + (ft % 2) * 128:
                                            (g % 3) * 2048 + ht * 256 + (ft % 2) * 128 + 128],
                                        xe[:, ht * HB + c * 512: ht * HB + c * 512 + 512],
                                        start=(ht == 0), stop=(ht == 7)),
                                        inc=semn, tag=f"s1_{u}_{ft}_{c}")
                                    if ht == 7:
                                        s1_pair.append((semn, v))
                        s1_last = s1_pair[-1]
                        s.mark(f"s1end{call}_{u}", s1_last[0])

                        nxt = u + 1
                        if nxt < nunits:
                            nsl, ne_ = divmod(nxt, NE)
                            if ne_ == 0 and nsl in pending_norm:
                                pending_norm.pop(nsl)()
                            emit_xe(nsl, ne_, s1_last)
                            emit_bcast(nsl, ne_)
                        if u >= 1:
                            s.wait_ms("scalar", f"s2end{call}_{u-1}", f"gwar{u}")
                        for ft in range(32):
                            for c in range(2):
                                ptok = s1_pair[ft * 2 + c]
                                s.wait("scalar", ptok[0], ptok[1], f"g{u}_{ft}_{c}")
                                s.emit("scalar", lambda e_, ft=ft, c=c, l=l, e=e, slot=pslots[ft * 2 + c]: e_.activation(
                                    hact[:, ft * HB + c * 512: ft * HB + c * 512 + 512],
                                    psU[:, slot * 512: slot * 512 + 512], AF.Gelu,
                                    bias=b1c[:, (l * 4 + e) * 32 + ft:(l * 4 + e) * 32 + ft + 1]),
                                    inc="GEL", tag=f"g{u}_{ft}_{c}")

                        s2_tile = []
                        tslots = []
                        for ho in range(8):
                            for c in range(2):
                                q = u * 16 + ho * 2 + c
                                if c == 0:
                                    s.wait_tok("tensor", w2_dw[u * 16 + ho * 2], f"s2w{u}_{ho}")
                                evt = len(evt_hist)
                                slot = evt % 5
                                if evt >= 5:
                                    s.wait("tensor", evt_hist[evt - 5][0], evt_hist[evt - 5][1],
                                           f"s2e{u}_{ho}_{c}")
                                evt_hist.append(("EVC", evc_base + q + 1))
                                tslots.append(slot)
                                s.wait("tensor", "GEL", gel_base + u * 64 + 63 + c, f"s2g{u}_{ho}_{c}")
                                for ft in range(32):
                                    if ft == 16:
                                        s.wait_tok("tensor", w2_dw[u * 16 + ho * 2 + 1], f"s2w{u}_{ho}b")
                                    semn = None
                                    if ft == 31:
                                        semn = "W2C" if c == 1 else "PE"
                                    elif ft == 15 and c == 1:
                                        semn = "W2C"
                                    ni = 1
                                    v = s.emit("tensor", lambda e_, g=(u * 16 + ho * 2 + ft // 16), ft=ft, c=c, slot=slot: e_.matmul(
                                        psU[:, slot * 512: slot * 512 + 512],
                                        w2b[:, (g % 3) * 2048 + (ft % 16) * 128:
                                            (g % 3) * 2048 + (ft % 16) * 128 + 128],
                                        hact[:, ft * HB + c * 512: ft * HB + c * 512 + 512],
                                        start=(ft == 0), stop=(ft == 31)),
                                        inc=semn, inc_n=ni, tag=f"s2_{u}_{ho}_{c}")
                                    if ft == 31:
                                        s2_tile.append((semn, v))
                                    del v
                        s.mark(f"s2end{call}_{u}", "W2C")

                        s.wait_ms("vector", f"bcs{call}_{sl}_{e}", f"ev{u}")
                        for ho in range(8):
                            for c in range(2):
                                q = u * 16 + ho * 2 + c
                                stok = s2_tile[ho * 2 + c]
                                s.wait("vector", stok[0], stok[1], f"ev{u}_{ho}_{c}")
                                tmpv = scr_t[:].bitcast(BF16)
                                s.emit("vector", lambda e_, ho=ho, c=c, q=q, l=l, e=e, par=e % 2, tmpv=tmpv, slot=tslots[ho * 2 + c]: e_.scalar_tensor_tensor(
                                    tmpv[:, 0:512],
                                    psU[:, slot * 512: slot * 512 + 512],
                                    b2c[:, (l * 4 + e) * 8 + ho:(l * 4 + e) * 8 + ho + 1],
                                    bcast[:, par * HB + c * 512: par * HB + c * 512 + 512],
                                    ALU.add, ALU.mult), inc="EVC", tag=f"ev1_{u}_{ho}_{c}")
                                s.emit("vector", lambda e_, ho=ho, c=c, half=half, tmpv=tmpv: e_.tensor_tensor(
                                    stream[:, ho * T + half * HB + c * 512: ho * T + half * HB + c * 512 + 512],
                                    stream[:, ho * T + half * HB + c * 512: ho * T + half * HB + c * 512 + 512],
                                    tmpv[:, 0:512], ALU.add),
                                    inc="V", tag=f"ev2_{u}_{ho}_{c}")
                        if e == 0 and 2 <= sl + 1 < nslots:
                            emit_chain(sl + 1)
                        if stop == f"u{u}":
                            s.mark(f"merge{call}", "V")
                            s.fence(f"ffn{call}_end")
                            return
                    s.mark(f"zupd{call}_{l}_{half}", "V")

                if dbg and call == dbg_call:
                    s.wait_ms("gpsimd", f"zupd{call}_{nslots//2-1}_{1}", "dbg")
                    dma("gpsimd", dbg_comb, comb_rows[:], "DW", "dbg")
                    dma("gpsimd", dbg_zhat, zhat[:], "DW", "dbg")
                    dma("gpsimd", dbg_xe, xe[:], "DW", "dbg")
                    dma("gpsimd", dbg_z, stream[:], "DW", "dbg")
                    s.fence("dbg_done")

                if dbg and call == dbg_call:
                    s.wait_fence("vector", "dbg_done", "dbgmg")
                xld = hact[:].bitcast(F32)
                s.wait_ms("sync", f"s2end{call}_{nunits-1}", "mgw")
                mg_tok = dma("sync", xld, xspill, "DS", f"mgld{call}")
                s.wait_tok("vector", mg_tok, "mg")
                s.wait_ms("vector", f"zupd{call}_{nslots//2 - 1}_{1}", "mgz")
                for ho in range(8):
                    for c in range(4):
                        s.emit("vector", lambda e_, ho=ho, c=c: e_.scalar_tensor_tensor(
                            stream[:, ho * T + c * 512: ho * T + c * 512 + 512],
                            stream[:, ho * T + c * 512: ho * T + c * 512 + 512], 0.5,
                            xld[:, ho * T + c * 512: ho * T + c * 512 + 512],
                            ALU.mult, ALU.add), inc="V", tag=f"mg{call}_{ho}_{c}")
                s.mark(f"merge{call}", "V")
            s.fence(f"ffn{call}_end")

        def emit_attention(pre_fence):
            stream_bf = stream[:].bitcast(BF16)

            def kc_sl(hslice, c0, w):
                pb, pw = hslice
                return stream[pb:pb + pw, c0 // 2: (c0 + w) // 2].bitcast(BF16)

            def vt_sl(c0, w):
                return stream[:, (16384 + c0) // 2: (16384 + c0 + w) // 2].bitcast(BF16)

            with ExitStack() as sc:
                Aa = sc.enter_context
                qC = Aa(nc.sbuf_tensor("qC", [128, 8 * T], BF16))
                qR = Aa(nc.sbuf_tensor("qR", [96, 6 * T], BF16))
                kR3 = Aa(nc.sbuf_tensor("kR3", [96, T], BF16))
                psP = Aa(nc.psum_tensor("psP", [128, 1024], F32))
                psT1 = Aa(nc.psum_tensor("psT1", [128, 1024], BF16))
                psT2 = Aa(nc.psum_tensor("psT2", [128, 1024], BF16))

                s.wait_fence("sync", pre_fence, "attn")
                tok_aspill = dma("sync", xspill, stream[:], "DS", "aspill")

                with ExitStack() as scA:
                    Ab = scA.enter_context
                    cKV = Ab(nc.sbuf_tensor("cKV", [128, 2 * T], BF16))
                    wdkvb = Ab(nc.sbuf_tensor("wdkvb", [128, 2 * 1024], BF16))
                    wukb = Ab(nc.sbuf_tensor("wukb", [128, 2048], BF16))
                    wuvb = Ab(nc.sbuf_tensor("wuvb", [128, 2048], BF16))
                    psp_user_v = [0, 0]
                    with ExitStack() as scB:
                        Ac = scB.enter_context
                        xhat = Ac(nc.sbuf_tensor("xhat", [128, 8 * HB], BF16))
                        cq8 = Ac(nc.sbuf_tensor("cq8", [128, 32 * 256], BF16))
                        wdqb = Ac(nc.sbuf_tensor("wdqb", [128, 2 * 1024], BF16))
                        wuqb = Ac(nc.sbuf_tensor("wuqb", [128, 2 * 1024], BF16))
                        wqrb = Ac(nc.sbuf_tensor("wqrb", [128, 2 * 768], BF16))
                        wkrb = Ac(nc.sbuf_tensor("wkrb", [128, 256], BF16))
                        qrp = Ac(nc.sbuf_tensor("qrp", [96, T], BF16))
                        azbf = Ac(nc.sbuf_tensor("azbf", [128, 8 * 128], BF16))
                        ascr = Ac(nc.sbuf_tensor("ascr", [128, 512], F32))
                        psCa = Ac(nc.psum_tensor("psCa", [128, 1024], F32))

                        Bа = {"zbf": azbf, "scr_t": ascr, "psC": psCa,
                              "z2war": [0]}

                        s.wait_fence("gpsimd", pre_fence, "attnw")
                        ctk = build_program._conv_toks
                        for nm in ("wdkv", "wuk", "wuv", "wkr", "wdq", "wuq", "wqr", "wo"):
                            s.wait_tok("sync", ctk[("a", nm)], "aconv")
                        res_toks = [
                            dma("sync", wdkvb[:, 0:1024], awq_bf["wdkv"][0], "DS", "adkv"),
                            dma("sync", wdkvb[:, 1024:2048], awq_bf["wdkv"][1], "DS", "adkv"),
                            dma("sync", wukb[:], awq_bf["wuk"], "DS", "auk"),
                            dma("sync", wuvb[:], awq_bf["wuv"], "DS", "auv"),
                            dma("sync", wkrb[:], awq_bf["wkr"], "DS", "akr"),
                        ]
                        tok_qrp = dma("sync", qrp[:], cst["qrope"].ap(), "DS", "aqrp")
                        s.wait_tok("vector", tok_qrp, "aqrp")
                        for tk in res_toks:
                            s.wait_tok("tensor", tk, "ares")

                        wdq_hist = []
                        wdq_pe = []
                        wuq_hist, wuq_pe = [], []
                        wqr_hist, wqr_pe = [], []

                        apsc_v = [0]
                        for half in range(2):
                            emit_ln_chain(
                                f"A{half}", half, Bа, [f"merge0"] if half == 0 else [],
                                router=None,
                                dst_fn=(lambda ht, ch: xhat[
                                    :, ht * HB + ch * 512: ht * HB + ch * 512 + 512]),
                                psc_v=apsc_v)
                            t0h = half * HB

                            for kt in range(2):
                                for ch in range(2):
                                    slot = (kt * 2 + ch) % 2
                                    s.wait("tensor", "V", psp_user_v[slot], f"ckv{kt}{ch}")
                                    s.wait_ms("tensor", f"A{half}_done", "ckvx")
                                    for ht in range(8):
                                        s.emit("tensor", lambda e, kt=kt, ch=ch, ht=ht, slot=slot, t0h=t0h: e.matmul(
                                            psP[:, slot * 512: slot * 512 + 512],
                                            wdkvb[:, kt * 1024 + ht * 128: kt * 1024 + ht * 128 + 128],
                                            xhat[:, ht * HB + ch * 512: ht * HB + ch * 512 + 512],
                                            start=(ht == 0), stop=(ht == 7)),
                                            inc=("PE" if ht == 7 else None), tag=f"ckv{kt}{ch}")
                                    pe = s.counts["PE"]
                                    s.wait("vector", "PE", pe, f"ckve{kt}{ch}")
                                    psp_user_v[slot] = s.emit("vector", lambda e, kt=kt, ch=ch, slot=slot, t0h=t0h: e.tensor_copy(
                                        cKV[:, kt * T + t0h + ch * 512: kt * T + t0h + ch * 512 + 512],
                                        psP[:, slot * 512: slot * 512 + 512]), inc="V", tag=f"ckve{kt}{ch}")
                            for ch in range(2):
                                slot = ch % 2
                                s.wait("tensor", "V", psp_user_v[slot], f"kr{ch}")
                                for ht in range(8):
                                    s.emit("tensor", lambda e, ch=ch, ht=ht, slot=slot, t0h=t0h: e.matmul(
                                        psP[0:32, slot * 512: slot * 512 + 512],
                                        wkrb[:, ht * 32: ht * 32 + 32],
                                        xhat[:, ht * HB + ch * 512: ht * HB + ch * 512 + 512],
                                        start=(ht == 0), stop=(ht == 7)),
                                        inc=("PE" if ht == 7 else None), tag=f"kr{ch}")
                                pe = s.counts["PE"]
                                s.wait("vector", "PE", pe, f"kre{ch}")
                                for bb in range(3):
                                    psp_user_v[slot] = s.emit("vector", lambda e, ch=ch, slot=slot, bb=bb, t0h=t0h: e.tensor_tensor(
                                        kR3[bb * 32:(bb + 1) * 32, t0h + ch * 512: t0h + ch * 512 + 512],
                                        psP[0:32, slot * 512: slot * 512 + 512],
                                        qrp[0:32, t0h + ch * 512: t0h + ch * 512 + 512],
                                        ALU.mult), inc="V", tag=f"kre{ch}_{bb}")

                            for q8 in range(4):
                                tw = t0h + q8 * 256
                                lw = q8 * 256
                                for dqt in range(32):
                                    gq = (half * 4 + q8) * 32 + dqt
                                    if gq >= 2:
                                        s.wait("sync", "PE", wdq_pe[gq - 2], f"wdqf{gq}")
                                    wdq_hist.append(dma(
                                        "sync", wdqb[:, (gq % 2) * 1024:(gq % 2) * 1024 + 1024],
                                        awq_bf["wdq"][dqt], "DS", f"wdq{gq}"))
                                    slot = dqt % 2
                                    s.wait_tok("tensor", wdq_hist[gq], f"cq{gq}")
                                    s.wait("tensor", "V", psp_user_v[slot], f"cq{gq}")
                                    for ht in range(8):
                                        s.emit("tensor", lambda e, gq=gq, ht=ht, lw=lw, slot=slot: e.matmul(
                                            psP[:, slot * 512: slot * 512 + 256],
                                            wdqb[:, (gq % 2) * 1024 + ht * 128:(gq % 2) * 1024 + ht * 128 + 128],
                                            xhat[:, ht * HB + lw: ht * HB + lw + 256],
                                            start=(ht == 0), stop=(ht == 7)),
                                            inc=("PE" if ht == 7 else None), tag=f"cq{gq}")
                                    wdq_pe.append(s.counts["PE"])
                                    s.wait("vector", "PE", s.counts["PE"], f"cqe{gq}")
                                    psp_user_v[slot] = s.emit("vector", lambda e, dqt=dqt, slot=slot: e.tensor_copy(
                                        cq8[:, dqt * 256: dqt * 256 + 256],
                                        psP[:, slot * 512: slot * 512 + 256]), inc="V", tag=f"cqe{gq}")
                                cq_ready_v = s.counts["V"]
                                for ho in range(8):
                                    gq = (half * 4 + q8) * 8 + ho
                                    slot = ho % 2
                                    s.wait("tensor", "V", cq_ready_v, f"uq{gq}")
                                    s.wait("tensor", "V", psp_user_v[slot], f"uq{gq}")
                                    for sb in range(4):
                                        gg = gq * 4 + sb
                                        if gg >= 2:
                                            s.wait("sync", "PE", wuq_pe[gg - 2], f"wuqf{gg}")
                                        wuq_hist.append(dma(
                                            "sync", wuqb[:, (gg % 2) * 1024:(gg % 2) * 1024 + 1024],
                                            awq_bf["wuq"][ho * 4 + sb], "DS", f"wuq{gg}"))
                                        s.wait_tok("tensor", wuq_hist[gg], f"uq{gg}")
                                        for dq in range(8):
                                            s.emit("tensor", lambda e, gg=gg, sb=sb, dq=dq, slot=slot: e.matmul(
                                                psP[:, slot * 512: slot * 512 + 256],
                                                wuqb[:, (gg % 2) * 1024 + dq * 128:(gg % 2) * 1024 + dq * 128 + 128],
                                                cq8[:, (sb * 8 + dq) * 256: (sb * 8 + dq) * 256 + 256],
                                                start=(sb == 0 and dq == 0), stop=(sb == 3 and dq == 7)),
                                                inc=("PE" if dq == 7 else None), tag=f"uq{gg}")
                                        wuq_pe.append(s.counts["PE"])
                                    s.wait("vector", "PE", s.counts["PE"], f"uqe{gq}")
                                    psp_user_v[slot] = s.emit("vector", lambda e, ho=ho, tw=tw, slot=slot: e.tensor_copy(
                                        qC[:, ho * T + tw: ho * T + tw + 256],
                                        psP[:, slot * 512: slot * 512 + 256]), inc="V", tag=f"uqe{gq}")
                                for gr in range(6):
                                    gq = (half * 4 + q8) * 6 + gr
                                    slot = gr % 2
                                    s.wait("tensor", "V", psp_user_v[slot], f"qr{gq}")
                                    for sb in range(4):
                                        gg = gq * 4 + sb
                                        if gg >= 2:
                                            s.wait("sync", "PE", wqr_pe[gg - 2], f"wqrf{gg}")
                                        wqr_hist.append(dma(
                                            "sync", wqrb[:, (gg % 2) * 768:(gg % 2) * 768 + 768],
                                            awq_bf["wqr"][gr * 4 + sb], "DS", f"wqr{gg}"))
                                        s.wait_tok("tensor", wqr_hist[gg], f"qr{gg}")
                                        for dq in range(8):
                                            s.emit("tensor", lambda e, gg=gg, sb=sb, dq=dq, slot=slot: e.matmul(
                                                psP[0:96, slot * 512: slot * 512 + 256],
                                                wqrb[:, (gg % 2) * 768 + dq * 96:(gg % 2) * 768 + dq * 96 + 96],
                                                cq8[:, (sb * 8 + dq) * 256: (sb * 8 + dq) * 256 + 256],
                                                start=(sb == 0 and dq == 0), stop=(sb == 3 and dq == 7)),
                                                inc=("PE" if dq == 7 else None), tag=f"qr{gg}")
                                        wqr_pe.append(s.counts["PE"])
                                    s.wait("vector", "PE", s.counts["PE"], f"qre{gq}")
                                    psp_user_v[slot] = s.emit("vector", lambda e, gr=gr, tw=tw, slot=slot: e.tensor_tensor(
                                        qR[0:96, gr * T + tw: gr * T + tw + 256],
                                        psP[0:96, slot * 512: slot * 512 + 256],
                                        qrp[0:96, tw: tw + 256], ALU.mult), inc="V", tag=f"qre{gq}")
                        s.mark("proj_done_pe", "PE")
                        s.mark("proj_done_v", "V")

                    with ExitStack() as scB2:
                        vtmp = scB2.enter_context(nc.sbuf_tensor("vtmp", [128, 512], BF16))
                        s.wait_tok("vector", tok_aspill, "kv")
                        pst_user_v = [0, 0]
                        psp2_user_v = psp_user_v
                        for ho in range(8):
                            for ch in range(4):
                                slot = ch % 2
                                s.wait("tensor", "V", psp2_user_v[slot], f"kc{ho}{ch}")
                                for kt in range(2):
                                    s.emit("tensor", lambda e, ho=ho, ch=ch, kt=kt, slot=slot: e.matmul(
                                        psP[:, slot * 512: slot * 512 + 512],
                                        wukb[:, kt * 1024 + ho * 128: kt * 1024 + ho * 128 + 128],
                                        cKV[:, kt * T + ch * 512: kt * T + ch * 512 + 512],
                                        start=(kt == 0), stop=(kt == 1)),
                                        inc=("PE" if kt == 1 else None), tag=f"kc{ho}{ch}")
                                s.wait("vector", "PE", s.counts["PE"], f"kce{ho}{ch}")
                                psp2_user_v[slot] = s.emit("vector", lambda e, ho=ho, ch=ch, slot=slot: e.tensor_copy(
                                    kc_sl((0, 128), ho * T + ch * 512, 512),
                                    psP[:, slot * 512: slot * 512 + 512]), inc="V", tag=f"kce{ho}{ch}")
                                slot2 = (ch + 1) % 2
                                s.wait("tensor", "V", psp2_user_v[slot2], f"v{ho}{ch}")
                                for kt in range(2):
                                    s.emit("tensor", lambda e, ho=ho, ch=ch, kt=kt, slot2=slot2: e.matmul(
                                        psP[:, slot2 * 512: slot2 * 512 + 512],
                                        wuvb[:, kt * 1024 + ho * 128: kt * 1024 + ho * 128 + 128],
                                        cKV[:, kt * T + ch * 512: kt * T + ch * 512 + 512],
                                        start=(kt == 0), stop=(kt == 1)),
                                        inc=("PE" if kt == 1 else None), tag=f"v{ho}{ch}")
                                s.wait("vector", "PE", s.counts["PE"], f"ve{ho}{ch}")
                                vv = s.emit("vector", lambda e, slot2=slot2: e.tensor_copy(
                                    vtmp[:], psP[:, slot2 * 512: slot2 * 512 + 512]), inc="V", tag=f"ve{ho}{ch}")
                                psp2_user_v[slot2] = vv
                                for j in range(4):
                                    tslot = j % 2
                                    pst = psT1 if tslot == 0 else psT2
                                    s.wait("tensor", "V", vv, f"vt{ho}{ch}{j}")
                                    s.wait("tensor", "V", pst_user_v[tslot], f"vt{ho}{ch}{j}")
                                    s.emit("tensor", lambda e, j=j, pst=pst: e.transpose(
                                        pst[:, 0:128], vtmp[:, j * 128: j * 128 + 128],
                                        id_b[:]), inc="PE", tag=f"vt{ho}{ch}{j}")
                                    s.wait("vector", "PE", s.counts["PE"], f"vte{ho}{ch}{j}")
                                    pst_user_v[tslot] = s.emit("vector", lambda e, ho=ho, ch=ch, j=j, pst=pst: e.tensor_copy(
                                        vt_sl((ch * 4 + j) * 1024 + ho * 128, 128),
                                        pst[:, 0:128]), inc="V", tag=f"vte{ho}{ch}{j}")
                        s.mark("kv_done_pe", "PE")
                        s.mark("kv_done_v", "V")

                with ExitStack() as scC:
                    Ad = scC.enter_context
                    yall = Ad(nc.sbuf_tensor("yall", [128, 16 * 1024], BF16))
                    srow = Ad(nc.sbuf_tensor("srow", [128, 2 * 2048], F32))
                    prow = Ad(nc.sbuf_tensor("prow", [128, 2 * 2048], BF16))
                    pTs = Ad(nc.sbuf_tensor("pTs", [128, 2 * 2048], BF16))
                    psS = Ad(nc.psum_tensor("psS", [128, 1024], F32))
                    psY1 = Ad(nc.psum_tensor("psY1", [128, 512], F32))
                    psY2 = Ad(nc.psum_tensor("psY2", [128, 512], F32))

                    s.wait_ms("tensor", "kv_done_v", "sc0")
                    pss_user = [("V", 0), ("V", 0)]
                    pst_user_v = [0, 0]
                    it_hist = []

                    def stage_a(it, h, qt):
                        sl_ = it % 2
                        hpb = (h % 2) * 64
                        hpt = h // 2
                        qgrp = h // 3
                        qoff = 32 * (h % 3)
                        L = (qt + 1) * 128
                        nfull = L // 512
                        chunks = [(j * 512, 512) for j in range(nfull)]
                        if L % 512:
                            chunks.append((nfull * 512, L % 512))
                        rec = {"sl": sl_, "L": L}
                        for j, (c0, w) in enumerate(chunks):
                            slot = j % 2
                            s.wait("tensor", pss_user[slot][0], pss_user[slot][1], f"sc{it}_{j}")
                            s.emit("tensor", lambda e, c0=c0, w=w, slot=slot, hpb=hpb, hpt=hpt, qt=qt: e.matmul(
                                psS[:, slot * 512: slot * 512 + w],
                                qC[hpb:hpb + 64, hpt * T + qt * 128: hpt * T + qt * 128 + 128],
                                kc_sl((hpb, 64), hpt * T + c0, w),
                                start=True, stop=False), tag=f"sc{it}_{j}")
                            pe = s.emit("tensor", lambda e, c0=c0, w=w, slot=slot, qoff=qoff, qgrp=qgrp, qt=qt: e.matmul(
                                psS[:, slot * 512: slot * 512 + w],
                                qR[qoff:qoff + 32, qgrp * T + qt * 128: qgrp * T + qt * 128 + 128],
                                kR3[qoff:qoff + 32, c0: c0 + w],
                                start=False, stop=True), inc="PE", tag=f"sc{it}_{j}b")
                            last = (j == len(chunks) - 1)
                            wcopy = w - 128 if last else w
                            s.wait("scalar", "PE", pe, f"sce{it}_{j}")
                            if wcopy > 0:
                                sc = s.emit("scalar", lambda e, c0=c0, w=wcopy, slot=slot, sl_=sl_: e.copy(
                                    srow[:, sl_ * 2048 + c0: sl_ * 2048 + c0 + w],
                                    psS[:, slot * 512: slot * 512 + w]), inc="S", tag=f"sce{it}_{j}")
                                pss_user[slot] = ("S", sc)
                            if last:
                                s.wait("vector", "PE", pe, f"scm{it}")
                                s.wait("vector", "S", s.counts["S"], f"scms{it}")
                                if it >= 2:
                                    s.wait("vector", "S", it_hist[it - 2]["exp_s"], f"scmw{it}")
                                vm = s.emit("vector", lambda e, c0=c0, w=w, slot=slot, sl_=sl_: e.tensor_tensor(
                                    srow[:, sl_ * 2048 + c0 + w - 128: sl_ * 2048 + c0 + w],
                                    psS[:, slot * 512 + w - 128: slot * 512 + w],
                                    tri[:], ALU.add), inc="V", tag=f"scm{it}")
                                pss_user[slot] = ("V", vm)
                        copies_s = s.counts["S"]
                        s.wait("vector", "S", copies_s, f"mx{it}")
                        s.emit("vector", lambda e, sl_=sl_, L=L: e.tensor_reduce(
                            smalls[:, 6 + sl_:7 + sl_], srow[:, sl_ * 2048: sl_ * 2048 + L],
                            AX, ALU.max, negate=True), inc="V", tag=f"mx{it}")
                        rec["mxs_v"] = s.emit("vector", lambda e, sl_=sl_: e.tensor_single_scalar(
                            smalls[:, 0 + sl_:1 + sl_], smalls[:, 6 + sl_:7 + sl_],
                            ISQ, ALU.mult), inc="V", tag=f"mxs{it}")
                        return rec

                    def stage_b(it, h, qt, rec):
                        sl_ = rec["sl"]
                        L = rec["L"]
                        s.wait("scalar", "V", rec["mxs_v"], f"ex{it}")
                        if it >= 2:
                            s.wait("scalar", "PE", it_hist[it - 2]["pt_pe"], f"pw{it}")
                        exp_s = s.emit("scalar", lambda e, sl_=sl_, L=L: e.activation(
                            prow[:, sl_ * 2048: sl_ * 2048 + L],
                            srow[:, sl_ * 2048: sl_ * 2048 + L], AF.Exp,
                            bias=smalls[:, 0 + sl_:1 + sl_], scale=ISQ,
                            accum_out=smalls[:, 2 + sl_:3 + sl_]), inc="S", tag=f"ex{it}")
                        s.wait("vector", "S", exp_s, f"rc{it}")
                        s.emit("vector", lambda e, sl_=sl_: e.reciprocal(
                            smalls[:, 4 + sl_:5 + sl_], smalls[:, 2 + sl_:3 + sl_]),
                            inc="V", tag=f"rc{it}")
                        psY = psY1 if sl_ == 0 else psY2
                        nkt = qt + 1
                        grpv = []
                        s.wait("tensor", "S", exp_s, f"pt{it}")
                        for g8 in range((nkt + 7) // 8):
                            k0 = g8 * 8
                            k1 = min(k0 + 8, nkt)
                            tslot = g8 % 2
                            pst = psT1 if tslot == 0 else psT2
                            s.wait("tensor", "V", pst_user_v[tslot], f"pt{it}_{g8}")
                            pe = None
                            for kt in range(k0, k1):
                                pe = s.emit("tensor", lambda e, kt=kt, k0=k0, sl_=sl_, pst=pst: e.transpose(
                                    pst[:, (kt - k0) * 128: (kt - k0) * 128 + 128],
                                    prow[:, sl_ * 2048 + kt * 128: sl_ * 2048 + kt * 128 + 128],
                                    id_b[:]), inc=("PE" if kt == k1 - 1 else None), tag=f"pt{it}_{kt}")
                            s.wait("vector", "PE", pe, f"pte{it}_{g8}")
                            pst_user_v[tslot] = s.emit("vector", lambda e, k0=k0, k1=k1, sl_=sl_, pst=pst: e.tensor_copy(
                                pTs[:, sl_ * 2048 + k0 * 128: sl_ * 2048 + k1 * 128],
                                pst[:, 0:(k1 - k0) * 128]), inc="V", tag=f"pte{it}_{g8}")
                            grpv.append((k1, s.counts["V"]))
                        pt_pe = s.counts["PE"]
                        if it >= 2:
                            s.wait("tensor", "V", it_hist[it - 2]["yev_v"], f"yw{it}")
                        y_pe = None
                        gi = 0
                        for kt in range(nkt):
                            if kt >= grpv[gi][0]:
                                gi += 1
                            s.wait("tensor", "V", grpv[gi][1], f"y{it}_{kt}")
                            y_pe = s.emit("tensor", lambda e, kt=kt, sl_=sl_, h=h, psY=psY, nkt=nkt: e.matmul(
                                psY[:, 0:64],
                                pTs[:, sl_ * 2048 + kt * 128: sl_ * 2048 + kt * 128 + 128],
                                vt_sl(kt * 1024 + h * 64, 64),
                                start=(kt == 0), stop=(kt == nkt - 1)),
                                inc=("PE" if kt == nkt - 1 else None), tag=f"y{it}_{kt}")
                        s.wait("vector", "PE", y_pe, f"ye{it}")
                        yev_v = s.emit("vector", lambda e, qt=qt, h=h, sl_=sl_, psY=psY: e.tensor_scalar_mul(
                            yall[:, qt * 1024 + h * 64: qt * 1024 + h * 64 + 64],
                            psY[:, 0:64], smalls[:, 4 + sl_:5 + sl_]), inc="V", tag=f"ye{it}")
                        it_hist.append({"exp_s": exp_s, "pt_pe": pt_pe, "y_pe": y_pe,
                                        "yev_v": yev_v})

                    iters = [(h, qt) for h in range(NH) for qt in range(16)]
                    recs = {}
                    for it, (h, qt) in enumerate(iters):
                        recs[it] = stage_a(it, h, qt)
                        if it >= 1:
                            ph, pqt = iters[it - 1]
                            stage_b(it - 1, ph, pqt, recs.pop(it - 1))
                    stage_b(len(iters) - 1, *iters[-1], recs.pop(len(iters) - 1))
                    s.mark("scores_done_pe", "PE")
                    s.mark("scores_done_v", "V")

                    yfm = qC
                    s.wait_ms("vector", "scores_done_pe", "yfm")
                    pst_user_v2 = [0, 0]
                    for qt in range(16):
                        for dt in range(8):
                            k = qt * 8 + dt
                            tslot = k % 2
                            pst = psT1 if tslot == 0 else psT2
                            s.wait("tensor", "V", it_hist[-1]["yev_v"], "yfm0")
                            s.wait("tensor", "V", pst_user_v[tslot], f"yf{k}")
                            s.wait("tensor", "V", pst_user_v2[tslot], f"yf{k}")
                            pe = s.emit("tensor", lambda e, qt=qt, dt=dt, pst=pst: e.transpose(
                                pst[:, 0:128], yall[:, qt * 1024 + dt * 128: qt * 1024 + dt * 128 + 128],
                                id_b[:]), inc="PE", tag=f"yf{k}")
                            s.wait("vector", "PE", pe, f"yfe{k}")
                            pst_user_v2[tslot] = s.emit("vector", lambda e, qt=qt, dt=dt, pst=pst: e.tensor_copy(
                                yfm[:, dt * T + qt * 128: dt * T + qt * 128 + 128],
                                pst[:, 0:128]), inc="V", tag=f"yfe{k}")
                    yfm_v = s.counts["V"]

                    wob = prow
                    xld2 = srow
                    wo_pe = []
                    psp_user_v3 = [0, 0]
                    s.wait("gpsimd", "S", it_hist[-1]["exp_s"], "wo0")
                    s.wait("gpsimd", "V", yfm_v, "wo0")
                    s.wait("sync", "S", it_hist[-1]["exp_s"], "wo0")
                    s.wait("sync", "V", yfm_v, "wo0")
                    mg_v = []
                    for ho in range(8):
                        if ho >= 2:
                            s.wait("gpsimd", "PE", wo_pe[ho - 2], f"wof{ho}")
                        wo_tok = dma("gpsimd", wob[:, (ho % 2) * 1024:(ho % 2) * 1024 + 1024],
                                     awq_bf["wo"][ho], "DW", f"wo{ho}")
                        if ho >= 2:
                            s.wait("sync", "V", mg_v[ho - 2], f"xlf{ho}")
                        xld_tok = dma("sync", xld2[:, (ho % 2) * 2048:(ho % 2) * 2048 + 2048],
                                      xspill[:, ho * T:(ho + 1) * T], "DS", f"xl{ho}")
                        s.wait_tok("tensor", wo_tok, f"wo{ho}")
                        s.wait("tensor", "V", yfm_v, "wo")
                        for c in range(4):
                            slot = c % 2
                            s.wait("tensor", "V", psp_user_v3[slot], f"woc{ho}{c}")
                            for dt in range(8):
                                s.emit("tensor", lambda e, ho=ho, c=c, dt=dt, slot=slot: e.matmul(
                                    psP[:, slot * 512: slot * 512 + 512],
                                    wob[:, (ho % 2) * 1024 + dt * 128:(ho % 2) * 1024 + dt * 128 + 128],
                                    yfm[:, dt * T + c * 512: dt * T + c * 512 + 512],
                                    start=(dt == 0), stop=(dt == 7)),
                                    inc=("PE" if dt == 7 else None), tag=f"wo{ho}{c}")
                            pe = s.counts["PE"]
                            s.wait("vector", "PE", pe, f"woe{ho}{c}")
                            s.wait_tok("vector", xld_tok, f"woe{ho}{c}")
                            v = s.emit("vector", lambda e, ho=ho, c=c, slot=slot: e.scalar_tensor_tensor(
                                stream[:, ho * T + c * 512: ho * T + c * 512 + 512],
                                psP[:, slot * 512: slot * 512 + 512], 1.0,
                                xld2[:, (ho % 2) * 2048 + c * 512:(ho % 2) * 2048 + c * 512 + 512],
                                ALU.mult, ALU.add), inc="V", tag=f"woe{ho}{c}")
                            psp_user_v3[slot] = v
                        wo_pe.append(s.counts["PE"])
                        mg_v.append(s.counts["V"])
                    s.mark("merge_attn", "V")
            s.fence("attn_end")

        emit_ffn(0, "postload")
        if do_attn:
            emit_attention("ffn0_end")
            if do_ffn2:
                emit_ffn(1, "attn_end")
                final_ms = "merge1"
            else:
                final_ms = "merge_attn"
        else:
            final_ms = "merge0"

        with ExitStack() as sco:
            obuf = sco.enter_context(nc.sbuf_tensor("obuf", [128, 2 * DM], F32))
            psTo = sco.enter_context(nc.psum_tensor("psTo", [128, 1024], F32))
            s.wait_ms("tensor", final_ms, "out")
            if dbg:
                s.wait_fence("vector", "dbg_done", "out")
                s.wait_fence("tensor", "dbg_done", "out")
            ds_hist = []
            ob_v = []
            for tt in range(16):
                sl = tt % 2
                for ht in range(8):
                    k = tt * 8 + ht
                    pb = k % 2
                    if k >= 2:
                        s.wait("tensor", "V", ob_v[k - 2], f"ot{k}")
                    if tt >= 2:
                        s.wait_tok("vector", ds_hist[tt - 2], f"ov{k}")
                    pe = s.emit("tensor", lambda e, tt=tt, ht=ht, pb=pb: e.transpose(
                        psTo[:, pb * 512: pb * 512 + 128],
                        stream[:, ht * T + tt * 128: ht * T + tt * 128 + 128],
                        id_f[:]), inc="PE", tag=f"ot{k}")
                    s.wait("vector", "PE", pe, f"ov{k}")
                    v = s.emit("vector", lambda e, sl=sl, ht=ht, pb=pb: e.tensor_copy(
                        obuf[:, sl * DM + ht * 128: sl * DM + ht * 128 + 128],
                        psTo[:, pb * 512: pb * 512 + 128]), inc="V", tag=f"ov{k}")
                    ob_v.append(v)
                s.wait("sync", "V", ob_v[-1], f"os{tt}")
                ds_hist.append(dma("sync", y_out[tt * 128:(tt + 1) * 128, :],
                                   obuf[:, sl * DM:(sl + 1) * DM], "DS", f"os{tt}"))
            for tk in ds_hist[-8:]:
                s.wait_tok("sync", tk, "final")

        s.check()
        s.dispatch()
    return nc



def prep_weight_maps(inp):
    m = {}
    m["wt"] = tile_ffn_weights(np.asarray(inp["L0_ffn_w1"], np.float32),
                               np.asarray(inp["L0_ffn_w2"], np.float32))
    g = np.asarray(inp["L0_ffn_ln_g"], np.float32)
    b = np.asarray(inp["L0_ffn_ln_b"], np.float32)
    m["gcol"] = np.ascontiguousarray(g.reshape(4, 4, 8, 128).transpose(3, 0, 1, 2).reshape(128, 128))
    m["bcol"] = np.ascontiguousarray(b.reshape(4, 4, 8, 128).transpose(3, 0, 1, 2).reshape(128, 128))
    b1 = np.asarray(inp["L0_ffn_b1"], np.float32)
    m["b1c"] = np.ascontiguousarray(b1.reshape(4, 4, 32, 128).transpose(3, 0, 1, 2).reshape(128, 512))
    b2 = np.asarray(inp["L0_ffn_b2"], np.float32)
    m["b2c"] = np.ascontiguousarray(b2.reshape(4, 4, 8, 128).transpose(3, 0, 1, 2).reshape(128, 128))
    rw = np.asarray(inp["L0_ffn_router_w"], np.float32)
    m["rwc"] = np.ascontiguousarray(rw.reshape(4, 8, 128, 4).transpose(2, 0, 1, 3).reshape(128, 128))
    m["rbc"] = np.ascontiguousarray(np.asarray(inp["L0_ffn_router_b"], np.float32).T)
    wdq = np.asarray(inp["L0_attn_W_DQ"], np.float32)
    m["wdq"] = np.ascontiguousarray(wdq.reshape(8, 128, 32, 128).transpose(2, 1, 0, 3).reshape(32, 128, 1024))
    wuq = np.asarray(inp["L0_attn_W_UQ"], np.float32)
    m["wuq"] = np.ascontiguousarray(wuq.reshape(4, 8, 128, 8, 128).transpose(3, 0, 2, 1, 4).reshape(32, 128, 1024))
    wqr = np.asarray(inp["L0_attn_W_QR"], np.float32)
    wqrp = np.zeros((4096, 576), np.float32)
    for h in range(NH):
        gg = h // 3
        off = (h % 3) * 32
        wqrp[:, gg * 96 + off: gg * 96 + off + 32] = wqr[:, h * 32:(h + 1) * 32]
    m["wqr"] = np.ascontiguousarray(wqrp.reshape(4, 8, 128, 6, 96).transpose(3, 0, 2, 1, 4).reshape(24, 128, 768))
    wdkv = np.asarray(inp["L0_attn_W_DKV"], np.float32)
    m["wdkv"] = np.ascontiguousarray(wdkv.reshape(8, 128, 2, 128).transpose(2, 1, 0, 3).reshape(2, 128, 1024))
    wuk = np.asarray(inp["L0_attn_W_UK"], np.float32)
    m["wuk"] = np.ascontiguousarray(wuk.reshape(2, 128, 8, 128).transpose(1, 0, 2, 3).reshape(128, 2048))
    wuv = np.asarray(inp["L0_attn_W_UV"], np.float32)
    m["wuv"] = np.ascontiguousarray(wuv.reshape(2, 128, 8, 128).transpose(1, 0, 2, 3).reshape(128, 2048))
    wkr = np.asarray(inp["L0_attn_W_KR"], np.float32)
    m["wkr"] = np.ascontiguousarray(wkr.reshape(8, 128, 32).transpose(1, 0, 2).reshape(128, 256))
    wo = np.asarray(inp["L0_attn_W_O"], np.float32)
    m["wo"] = np.ascontiguousarray(wo.reshape(8, 128, 8, 128).transpose(2, 1, 0, 3).reshape(8, 128, 1024))
    return m


def kernel(**inputs):
    x = np.asarray(inputs["L0_attn_x"], np.float32)
    wm = prep_weight_maps(inputs)
    nc = build_program()
    in_maps = []
    for i in range(8):
        d = dict(wm)
        d["x_in"] = np.ascontiguousarray(x[i])
        in_maps.append(d)
    res = run_bass_kernel_spmd(nc, in_maps, core_ids=list(range(8)))
    out = np.stack([np.asarray(r["y_out"], np.float32) for r in res.results], axis=0)
    return out


if __name__ == "__main__":
    import time
    t0 = time.time()
    nc = build_program()
    print(f"build ok: {time.time()-t0:.1f}s n_inst={nc.n_instructions()}")
